# version 05
# baseline (speedup 1.0000x reference)
import contextlib
import numpy as np
import concourse.bass as bass
import concourse.mybir as mybir
from concourse.bass_utils import run_bass_kernel_spmd

F32 = mybir.dt.float32
BF16 = mybir.dt.bfloat16
I32 = mybir.dt.int32
AF = mybir.ActivationFunctionType
ALU = mybir.AluOpType

NCORES = 8
B, T, C, I = 4, 2048, 2048, 8192
TL = T // 2
CK = C // 128
IK = I // 128
GI = 32
NG = IK // GI
NF = 512
NT = TL // NF
EPS = 1e-5

VOFF = dict(tf=0, tmk=16, tmv=32, tmr=48, xx=64, aa=80, bb=96, pp=112,
            iseven=128, usec=129,
            sgs0=132, sus0=196, sds0=260, sgs1=276, sus1=340, sds1=404, end=420)
VW = 420

ENGINES = ("sync", "scalar", "vector", "tensor", "gpsimd")


class KB:

    def __init__(self):
        self.ops = {e: [] for e in ENGINES}
        self.cnt = {}
        self.h = {}

    def sem(self, name):
        self.cnt.setdefault(name, 0)

    def inc(self, name, n=1):
        self.cnt[name] += n
        return self.cnt[name]

    def cur(self, name):
        return self.cnt[name]

    def op(self, eng, f):
        self.ops[eng].append(f)

    def wait(self, eng, sname, val):
        if val <= 0:
            return
        self.op(eng, lambda e, s=sname, v=val: e.wait_ge(self.h[s], v))


class Banks:
    def __init__(self, kb, handles):
        self.kb = kb
        self.pb = handles
        self.cond = [[] for _ in handles]

    def acquire(self, b):
        for (s, v) in self.cond[b]:
            self.kb.wait("tensor", s, v)
        self.cond[b] = []

    def release(self, b, sname, val):
        self.cond[b].append((sname, val))


def build_program(dbg=False, stages=6, lnsub=5, attnsub=6, attnj=CK):
    nc = bass.Bass("TRN2", target_bir_lowering=False)

    x_in = nc.dram_tensor("x_in", [128, CK * TL], F32, kind="ExternalInput").ap()
    vecs = nc.dram_tensor("vecs", [128, VW], F32, kind="ExternalInput").ap()
    wk_t = nc.dram_tensor("wk_t", [CK, 128, 2048], F32, kind="ExternalInput").ap()
    wv_t = nc.dram_tensor("wv_t", [CK, 128, 2048], F32, kind="ExternalInput").ap()
    wr_t = nc.dram_tensor("wr_t", [CK, 128, 2048], F32, kind="ExternalInput").ap()
    wo_t = nc.dram_tensor("wo_t", [CK, 128, 2048], F32, kind="ExternalInput").ap()
    w1_t = nc.dram_tensor("w1_t", [2 * CK, 128, 2048], F32, kind="ExternalInput").ap()
    wg = [nc.dram_tensor(f"wg{l}", [IK, 128, 2048], I32, kind="ExternalInput").ap() for l in range(2)]
    wu = [nc.dram_tensor(f"wu{l}", [IK, 128, 2048], I32, kind="ExternalInput").ap() for l in range(2)]
    wd = [nc.dram_tensor(f"wd{l}", [NG, 16, 128, GI * 128], I32, kind="ExternalInput").ap() for l in range(2)]
    o_x = nc.dram_tensor("o_x", [128, CK * TL], F32, kind="ExternalOutput").ap()
    o_dbg = None
    if dbg:
        o_dbg = nc.dram_tensor("o_dbg", [3, 128, CK * TL], F32, kind="ExternalOutput").ap()

    cc_in1 = nc.dram_tensor("cc_in1", [128, 8], F32)
    cc_out1 = nc.dram_tensor("cc_out1", [128, 8], F32)
    cc_in2 = nc.dram_tensor("cc_in2", [128, 8], F32)
    cc_out2 = nc.dram_tensor("cc_out2", [128, 8], F32)

    BASE = 16512
    A = lambda name, shape, dtype, off: nc.alloc_sbuf_tensor_at(name, shape, dtype, offset=BASE + off)

    xT = A("xT", [128, CK * TL], F32, 0)
    xn = A("xn", [128, CK * TL], BF16, 65536)
    WS = 98304
    acc = A("acc", [128, TL], F32, WS + 0)
    acc2 = A("acc2", [128, TL], F32, WS + 4096)
    sqtmp = A("sqtmp", [128, TL], F32, WS + 8192)
    lnscr = A("lnscr", [128, TL], F32, WS + 12288)
    mrow = A("mrow", [1, TL], F32, WS + 16384)
    rowA = A("rowA", [1, TL], F32, WS + 20480)
    rowB = A("rowB", [1, TL], F32, WS + 24576)
    mixbuf = A("mixbuf", [128, CK * TL], BF16, WS + 0)
    ybuf = A("ybuf", [128, CK * TL], BF16, WS + 32768)
    hbuf = A("hbuf", [128, GI * TL], BF16, WS + 0)
    slotq = A("slotq", [128, 8 * TL], F32, WS + 0)
    slotg = A("slotg", [128, 8 * TL], F32, WS + 32768)

    STG = 163840
    ringA = [A(f"ringA{i}", [128, 2048], BF16, STG + i * 4096) for i in range(3)]
    ringD = [A(f"ringD{i}", [128, 4096], BF16, STG + 12288 + i * 8192) for i in range(2)]
    scanscr = A("scanscr", [128, TL], F32, STG + 12288)
    scanscr2 = A("scanscr2", [128, TL], F32, STG + 16384)
    csb = A("csb", [128, TL], BF16, STG + 28672)

    MISC = 196608
    scr2 = [A(f"scr2_{i}", [128, NF], F32, MISC + i * 2048) for i in range(2)]
    vperm = A("vperm", [128, VW], F32, MISC + 4096)
    mo = MISC + 4096 + 1696

    def MA(name, shape, dtype, sz):
        nonlocal mo
        t = A(name, shape, dtype, mo)
        mo += sz
        return t
    ck_t = MA("ck_t", [128, CK], F32, 64)
    cv_t = MA("cv_t", [128, CK], F32, 64)
    cr_t = MA("cr_t", [128, CK], F32, 64)
    kcol = MA("kcol", [128, CK], F32, 64)
    vcol = MA("vcol", [128, CK], F32, 64)
    rcol = MA("rcol", [128, CK], F32, 64)
    cA = MA("cA", [128, CK], F32, 64)
    cB = MA("cB", [128, CK], F32, 64)
    cC = MA("cC", [128, CK], F32, 64)
    cD = MA("cD", [128, CK], F32, 64)
    lastcols = MA("lastcols", [128, CK], F32, 64)
    mixcol_b = MA("mixcol_b", [128, CK], BF16, 32)
    ccsb1 = MA("ccsb1", [128, 8], F32, 32)
    ccsb2 = MA("ccsb2", [128, 8], F32, 32)
    ccres1 = MA("ccres1", [128, 8], F32, 32)
    ccres2 = MA("ccres2", [128, 8], F32, 32)
    cbuf1 = MA("cbuf1", [128, 8], F32, 32)
    cbuf2 = MA("cbuf2", [128, 8], F32, 32)
    epsc = MA("epsc", [128, 1], F32, 32)
    ones_col_f = MA("ones_col_f", [128, 1], F32, 32)
    ones_col_b = MA("ones_col_b", [128, 1], BF16, 32)
    ones_row_f = MA("ones_row_f", [1, 128], F32, 512)
    ones_row_b = MA("ones_row_b", [1, 128], BF16, 256)
    csrow = MA("csrow", [1, TL], BF16, 2048)
    hs_hi = MA("hs_hi", [1, TL], BF16, 2048)
    hs_lo = MA("hs_lo", [1, TL], BF16, 2048)
    assert mo - MISC <= 229344 - BASE - MISC, mo

    def vcolv(name, j=0):
        off = VOFF[name]
        return vperm[:, off + j:off + j + 1]

    pb = [nc.alloc_psum_tensor(f"pb{i}", [128, NF], F32) for i in range(8)]

    kb = KB()
    for s in ("xl", "wq", "mm", "evS", "evV", "rhs", "scl", "vcc", "ccs", "st",
              "vsn", "wqA0", "wqA1", "wqA2", "wqD0", "wqD1"):
        kb.sem(s)
    banks = Banks(kb, pb)
    scr2_cred = [0, 0]

    def dma(eng, out_ap, in_ap, sname):
        v = kb.inc(sname, 16)
        kb.op(eng, lambda e, o=out_ap, i=in_ap, s=sname: e.dma_start(out=o, in_=i).then_inc(kb.h[s], 16))
        return v

    def t_mm(out, lhsT, rhs, start, stop, inc=None):
        if inc is None:
            kb.op("tensor", lambda e, o=out, l=lhsT, r=rhs, s=start, p=stop:
                  e.matmul(o, l, r, start=s, stop=p))
            return None
        v = kb.inc(inc)
        kb.op("tensor", lambda e, o=out, l=lhsT, r=rhs, s=start, p=stop, sn=inc:
              e.matmul(o, l, r, start=s, stop=p).then_inc(kb.h[sn], 1))
        return v

    v_edge = [None]

    def v_ser(f, inc=None):
        if v_edge[0] is not None:
            kb.wait("vector", v_edge[0][0], v_edge[0][1])
        s = inc or "vsn"
        v = kb.inc(s)
        kb.op("vector", lambda e, f=f, s=s: f(e).then_inc(kb.h[s], 1))
        v_edge[0] = (s, v)
        return v

    def v_op(f, inc=None):
        if inc is None:
            kb.op("vector", f)
            return None
        v = kb.inc(inc)
        kb.op("vector", lambda e, f=f, s=inc: f(e).then_inc(kb.h[s], 1))
        return v

    ringA_loads = [0]
    ringA_marks = []

    def ringA_load(src_ap):
        n = ringA_loads[0]
        slot = ringA[n % 3]
        sname = f"wqA{n % 3}"
        if n >= 3:
            kb.wait("gpsimd", "mm", ringA_marks[n - 3])
        v = dma("gpsimd", slot[:, :], src_ap, sname)
        ringA_loads[0] += 1
        return slot, (sname, v)

    def ringA_consumed():
        ringA_marks.append(kb.cur("mm"))

    ringD_loads = [0]
    ringD_marks = []

    def ringD_load(src_ap):
        n = ringD_loads[0]
        slot = ringD[n % 2]
        sname = f"wqD{n % 2}"
        if n >= 2:
            kb.wait("gpsimd", "mm", ringD_marks[n - 2])
        v = dma("gpsimd", slot[:, :], src_ap, sname)
        ringD_loads[0] += 1
        return slot, (sname, v)

    def ringD_consumed():
        ringD_marks.append(kb.cur("mm"))

    for q in range(4):
        dma("sync", xT[:, q * 4096:(q + 1) * 4096], x_in[:, q * 4096:(q + 1) * 4096], "xl")
    v_vecs = dma("sync", vperm[:, :], vecs, "xl")

    kb.op("vector", lambda e: e.memset(epsc[:, :], EPS))
    kb.op("vector", lambda e: e.memset(ones_col_f[:, :], 1.0))
    kb.op("vector", lambda e: e.memset(ones_col_b[:, :], 1.0))
    kb.op("vector", lambda e: e.memset(ones_row_f[:, :], 1.0))
    kb.op("vector", lambda e: e.memset(ones_row_b[:, :], 1.0))
    kb.wait("vector", "xl", v_vecs)
    for nm, dst in (("tmk", ck_t), ("tmv", cv_t), ("tmr", cr_t)):
        o = VOFF[nm]
        xxs = vperm[:, VOFF["xx"]:VOFF["xx"] + CK]
        v_ser(lambda e, o=o, xxs=xxs: e.tensor_tensor(
            out=cD[:, :], in0=xxs, in1=vperm[:, o:o + CK], op=ALU.mult))
        v_ser(lambda e, d=dst, xxs=xxs: e.tensor_tensor(
            out=d[:, :], in0=xxs, in1=cD[:, :], op=ALU.subtract))
    v_xloaded = kb.cur("xl")

    def ln_stats_hook(o):
        if o == 0:
            for b in range(4, 8):
                for (s, v) in banks.cond[b]:
                    kb.wait("vector", s, v)
                banks.cond[b] = []
        for n in range(NT):
            xsl = xT[:, o * TL + n * NF:o * TL + (n + 1) * NF]
            if o == 0:
                kb.op("vector", lambda e, n=n, xsl=xsl: e.tensor_scalar(
                    out=pb[4 + n][:, :], in0=xsl, scalar1=1.0, scalar2=None, op0=ALU.mult))
            else:
                kb.op("vector", lambda e, n=n, xsl=xsl: e.tensor_tensor(
                    out=pb[4 + n][:, :], in0=pb[4 + n][:, :], in1=xsl, op=ALU.add))
            kb.op("vector", lambda e, n=n, xsl=xsl: e.tensor_tensor(
                out=scr2[n][:, :], in0=xsl, in1=xsl, op=ALU.mult))
            if o == 0:
                kb.op("vector", lambda e, n=n: e.tensor_scalar(
                    out=pb[6 + n][:, :], in0=scr2[n][:, :], scalar1=1.0, scalar2=None, op0=ALU.mult))
            else:
                kb.op("vector", lambda e, n=n: e.tensor_tensor(
                    out=pb[6 + n][:, :], in0=pb[6 + n][:, :], in1=scr2[n][:, :], op=ALU.add))
        if o == 15:
            vv = v_op(lambda e: e.tensor_copy(out=lnscr[:, 0:1], in_=pb[4][:, 0:1]), inc="evV")
            scr2_cred[0] = vv
            scr2_cred[1] = vv

    def emit_layernorm(layer, fused=False):
        if layer == 0:
            kb.wait("vector", "xl", v_xloaded)
        if fused:
            for n in range(NT):
                kb.op("vector", lambda e, n=n: e.tensor_copy(
                    out=acc[:, n * NF:(n + 1) * NF], in_=pb[4 + n][:, :]))
            v_acc = None
            for n in range(NT):
                v_acc = v_op(lambda e, n=n: e.tensor_copy(
                    out=acc2[:, n * NF:(n + 1) * NF], in_=pb[6 + n][:, :]),
                    inc="rhs" if n == NT - 1 else None) or v_acc
            for b in range(4, 8):
                banks.release(b, "rhs", v_acc)
            kb.wait("tensor", "rhs", v_acc)
            for bnk in range(4):
                banks.acquire(bnk)
            for n in range(NT):
                t_mm(pb[n][0:1, :], ones_col_f[:, :], acc[:, n * NF:(n + 1) * NF], True, True)
            m_stats = None
            for n in range(NT):
                m_stats = t_mm(pb[2 + n][0:1, :], ones_col_f[:, :], acc2[:, n * NF:(n + 1) * NF],
                               True, True, inc="mm" if n == NT - 1 else None) or m_stats
            return _ln_tail(m_stats)
        kb.op("vector", lambda e: e.tensor_tensor(out=acc[:, :], in0=xT[:, 0:TL], in1=xT[:, TL:2 * TL], op=ALU.add))
        for k in range(2, CK):
            kb.op("vector", lambda e, k=k: e.tensor_tensor(
                out=acc[:, :], in0=acc[:, :], in1=xT[:, k * TL:(k + 1) * TL], op=ALU.add))
        kb.op("vector", lambda e: e.tensor_tensor(out=acc2[:, :], in0=xT[:, 0:TL], in1=xT[:, 0:TL], op=ALU.mult))
        v_acc = None
        for k in range(1, CK):
            kb.op("vector", lambda e, k=k: e.tensor_tensor(
                out=sqtmp[:, :], in0=xT[:, k * TL:(k + 1) * TL], in1=xT[:, k * TL:(k + 1) * TL], op=ALU.mult))
            v_acc = v_op(lambda e: e.tensor_tensor(out=acc2[:, :], in0=acc2[:, :], in1=sqtmp[:, :], op=ALU.add),
                         inc="rhs" if k == CK - 1 else None) or v_acc
        if lnsub < 2:
            return v_acc

        kb.wait("tensor", "rhs", v_acc)
        for bnk in range(4):
            banks.acquire(bnk)
        for n in range(NT):
            t_mm(pb[n][0:1, :], ones_col_f[:, :], acc[:, n * NF:(n + 1) * NF], True, True)
        m_stats = None
        for n in range(NT):
            m_stats = t_mm(pb[2 + n][0:1, :], ones_col_f[:, :], acc2[:, n * NF:(n + 1) * NF],
                           True, True, inc="mm" if n == NT - 1 else None) or m_stats
        if lnsub < 3:
            for bnk in range(4):
                banks.release(bnk, "mm", m_stats)
            return v_acc

        return _ln_tail(m_stats)

    def _ln_tail(m_stats):
        kb.wait("vector", "mm", m_stats)
        for n in range(NT):
            kb.op("vector", lambda e, n=n: e.tensor_scalar(
                out=mrow[0:1, n * NF:(n + 1) * NF], in0=pb[n][0:1, :], scalar1=1.0 / C, scalar2=None, op0=ALU.mult))
            kb.op("vector", lambda e, n=n: e.tensor_scalar(
                out=rowA[0:1, n * NF:(n + 1) * NF], in0=pb[2 + n][0:1, :], scalar1=1.0 / C, scalar2=None, op0=ALU.mult))
        kb.op("vector", lambda e: e.tensor_tensor(out=rowB[0:1, :], in0=mrow[0:1, :], in1=mrow[0:1, :], op=ALU.mult))
        v_var = v_op(lambda e: e.tensor_tensor(out=rowA[0:1, :], in0=rowA[0:1, :], in1=rowB[0:1, :], op=ALU.subtract),
                     inc="rhs")

        kb.wait("scalar", "rhs", v_var)
        s_sqrt = kb.inc("scl")
        kb.op("scalar", lambda e: e.activation(
            out=rowB[0:1, :], in_=rowA[0:1, :], func=AF.Sqrt, bias=epsc[0:1, :]).then_inc(kb.h["scl"], 1))
        kb.wait("vector", "scl", s_sqrt)
        v_rstd = v_op(lambda e: e.reciprocal(out=rowA[0:1, :], in_=rowB[0:1, :]), inc="rhs")
        if lnsub < 4:
            for bnk in range(4):
                banks.release(bnk, "rhs", v_rstd)
            return v_rstd

        kb.wait("tensor", "rhs", v_rstd)
        for bnk in range(4, 8):
            banks.acquire(bnk)
        for n in range(NT):
            t_mm(pb[4 + n][:, :], ones_row_f[:, :], mrow[0:1, n * NF:(n + 1) * NF], True, True)
        m_bc = None
        for n in range(NT):
            m_bc = t_mm(pb[6 + n][:, :], ones_row_f[:, :], rowA[0:1, n * NF:(n + 1) * NF],
                        True, True, inc="mm" if n == NT - 1 else None) or m_bc
        if lnsub < 5:
            for bnk in range(8):
                banks.release(bnk, "mm", m_bc)
            return v_rstd

        kb.wait("vector", "mm", m_bc)
        v_xn = None
        for k in range(CK):
            for n in range(NT):
                sl = slice(k * TL + n * NF, k * TL + (n + 1) * NF)
                kb.op("vector", lambda e, sl=sl, n=n: e.tensor_tensor(
                    out=lnscr[:, 0:NF], in0=xT[:, sl], in1=pb[4 + n][:, :], op=ALU.subtract))
                v_xn = v_op(lambda e, sl=sl, n=n: e.tensor_tensor(
                    out=xn[:, sl], in0=lnscr[:, 0:NF], in1=pb[6 + n][:, :], op=ALU.mult),
                    inc="rhs" if (k == CK - 1 and n == NT - 1) else None) or v_xn
        for bnk in range(8):
            banks.release(bnk, "rhs", v_xn)
        return v_xn

    def emit_l0_attn():
        def emit_mix(tmname, coff):
            for k in range(CK):
                r = v_op(lambda e, k=k: e.tensor_scalar(
                    out=mixbuf[:, k * TL:(k + 1) * TL], in0=xn[:, k * TL:(k + 1) * TL],
                    scalar1=vcolv(tmname, k), scalar2=coff[:, k:k + 1], op0=ALU.mult, op1=ALU.add),
                    inc="rhs" if k == CK - 1 else None)
            return r

        for c in range(CK):
            v_op(lambda e, c=c: e.tensor_scalar(
                out=mixcol_b[:, c:c + 1], in0=xn[:, c * TL:c * TL + 1],
                scalar1=vcolv("tmk", c), scalar2=ck_t[:, c:c + 1], op0=ALU.mult, op1=ALU.add),
                inc="rhs" if c == CK - 1 else None)
        v_mix = kb.cur("rhs")

        def phase_matmuls(w_dram, consume, pre=None):
            for j in range(attnj):
                if pre is not None:
                    pre(j)
                slot, vw = ringA_load(w_dram[j])
                kb.wait("tensor", vw[0], vw[1])
                if j == 0:
                    kb.wait("tensor", "rhs", kb.cur("rhs"))
                bA = (2 * j) % 4
                bB = (2 * j + 1) % 4
                banks.acquire(bA)
                banks.acquire(bB)
                m = None
                for c in range(CK):
                    t_mm(pb[bA][:, :], slot[:, c * 128:(c + 1) * 128],
                         mixbuf[:, c * TL:c * TL + NF], c == 0, c == CK - 1)
                    m = t_mm(pb[bB][:, :], slot[:, c * 128:(c + 1) * 128],
                             mixbuf[:, c * TL + NF:c * TL + 2 * NF],
                             c == 0, c == CK - 1, inc="mm" if c == CK - 1 else None) or m
                consume(j, 0, bA, m)
                consume(j, 1, bB, m)
                ringA_consumed()

        if attnsub < 3:
            return kb.cur("mm")

        emit_mix("tmv", cv_t)

        def kcol_step(j):
            slot, vw = ringA_load(wk_t[j])
            kb.wait("tensor", vw[0], vw[1])
            if j == 0:
                kb.wait("tensor", "rhs", v_mix)
            bnk = 4 + j % 2
            banks.acquire(bnk)
            m = None
            for c in range(CK):
                m = t_mm(pb[bnk][:, 0:1], slot[:, c * 128:(c + 1) * 128],
                         mixcol_b[:, c:c + 1], c == 0, c == CK - 1,
                         inc="mm" if c == CK - 1 else None) or m
            kb.wait("vector", "mm", m)
            vv = v_op(lambda e, bnk=bnk, j=j: e.tensor_copy(
                out=kcol[:, j:j + 1], in_=pb[bnk][:, 0:1]), inc="evV")
            banks.release(bnk, "evV", vv)
            ringA_consumed()

        def v_consume(j, n, bnk, m):
            if n == 0:
                kb.wait("vector", "mm", m)
                vv = v_op(lambda e, bnk=bnk, j=j: e.tensor_copy(
                    out=vcol[:, j:j + 1], in_=pb[bnk][:, 0:1]), inc="evV")
                banks.release(bnk, "evV", vv)
            kb.wait("scalar", "mm", m)
            sl = slice(j * TL + n * NF, j * TL + (n + 1) * NF)
            sv = kb.inc("evS")
            kb.op("scalar", lambda e, bnk=bnk, sl=sl: e.activation(
                out=ybuf[:, sl], in_=pb[bnk][:, :], func=AF.Copy).then_inc(kb.h["evS"], 1))
            banks.release(bnk, "evS", sv)

        phase_matmuls(wv_t, v_consume, pre=kcol_step)
        m_v_done = kb.cur("mm")
        e_v_sig = kb.cur("evS")
        if attnsub < 4:
            return kb.cur("mm")

        kb.wait("vector", "mm", m_v_done)
        kb.wait("vector", "evS", e_v_sig)
        emit_mix("tmr", cr_t)

        def r_consume(j, n, bnk, m):
            scrs = scr2[n]
            kb.wait("scalar", "mm", m)
            kb.wait("scalar", "evV", scr2_cred[n])
            sv = kb.inc("evS")
            kb.op("scalar", lambda e, bnk=bnk, scrs=scrs: e.activation(
                out=scrs[:, :], in_=pb[bnk][:, :], func=AF.Sigmoid).then_inc(kb.h["evS"], 1))
            banks.release(bnk, "evS", sv)
            kb.wait("vector", "evS", sv)
            if n == 0:
                kb.op("vector", lambda e, scrs=scrs, j=j: e.tensor_copy(
                    out=rcol[:, j:j + 1], in_=scrs[:, 0:1]))
            sl = slice(j * TL + n * NF, j * TL + (n + 1) * NF)
            vv = v_op(lambda e, scrs=scrs, sl=sl: e.tensor_tensor(
                out=ybuf[:, sl], in0=ybuf[:, sl], in1=scrs[:, :], op=ALU.mult), inc="evV")
            scr2_cred[n] = vv

        phase_matmuls(wr_t, r_consume)
        if attnsub < 5:
            return kb.cur("mm")

        aat = vperm[:, VOFF["aa"]:VOFF["aa"] + CK]
        bbt = vperm[:, VOFF["bb"]:VOFF["bb"] + CK]
        ppt = vperm[:, VOFF["pp"]:VOFF["pp"] + CK]
        tft = vperm[:, VOFF["tf"]:VOFF["tf"] + CK]
        v_ser(lambda e: e.tensor_tensor(out=cA[:, :], in0=kcol[:, :], in1=tft, op=ALU.add))
        v_ser(lambda e: e.tensor_tensor(out=cB[:, :], in0=cA[:, :], in1=ppt, op=ALU.max))
        v_ser(lambda e: e.tensor_tensor(out=cC[:, :], in0=ppt, in1=cB[:, :], op=ALU.subtract))
        v_ser(lambda e: e.tensor_tensor(out=cA[:, :], in0=cA[:, :], in1=cB[:, :], op=ALU.subtract))
        v_ser(lambda e: e.tensor_scalar(
            out=cC[:, :], in0=cC[:, :], scalar1=-100.0, scalar2=None, op0=ALU.max))
        v_c1 = v_ser(lambda e: e.tensor_scalar(
            out=cA[:, :], in0=cA[:, :], scalar1=-100.0, scalar2=None, op0=ALU.max), inc="rhs")
        kb.wait("scalar", "rhs", v_c1)
        s_exp = kb.inc("scl")
        kb.op("scalar", lambda e: e.activation(out=cC[:, :], in_=cC[:, :], func=AF.Exp))
        kb.op("scalar", lambda e: e.activation(out=cA[:, :], in_=cA[:, :], func=AF.Exp).then_inc(kb.h["scl"], 1))
        kb.wait("vector", "scl", s_exp)
        v_ser(lambda e: e.tensor_tensor(out=cB[:, :], in0=cC[:, :], in1=aat, op=ALU.mult))
        v_ser(lambda e: e.tensor_tensor(out=cD[:, :], in0=cA[:, :], in1=vcol[:, :], op=ALU.mult))
        v_ser(lambda e: e.tensor_tensor(out=cB[:, :], in0=cB[:, :], in1=cD[:, :], op=ALU.add))
        v_ser(lambda e: e.tensor_tensor(out=cC[:, :], in0=cC[:, :], in1=bbt, op=ALU.mult))
        v_ser(lambda e: e.tensor_tensor(out=cC[:, :], in0=cC[:, :], in1=cA[:, :], op=ALU.add))
        v_ser(lambda e: e.reciprocal(out=cC[:, :], in_=cC[:, :]))
        v_ser(lambda e: e.tensor_tensor(out=cB[:, :], in0=cB[:, :], in1=cC[:, :], op=ALU.mult))
        v_ser(lambda e: e.tensor_tensor(out=cB[:, :], in0=cB[:, :], in1=rcol[:, :], op=ALU.mult))
        for j in range(CK):
            v_ser(lambda e, j=j: e.tensor_copy(out=ybuf[:, j * TL:j * TL + 1], in_=cB[:, j:j + 1]),
                  inc="rhs" if j == CK - 1 else None)
        if attnsub < 6:
            return kb.cur("mm")

        def o_consume(j, n, bnk, m):
            kb.wait("vector", "mm", m)
            sl = slice(j * TL + n * NF, j * TL + (n + 1) * NF)
            vv = v_op(lambda e, bnk=bnk, sl=sl: e.tensor_tensor(
                out=xT[:, sl], in0=xT[:, sl], in1=pb[bnk][:, :], op=ALU.add), inc="evV")
            banks.release(bnk, "evV", vv)

        def out_matmuls():
            for j in range(CK):
                slot, vw = ringA_load(wo_t[j])
                kb.wait("tensor", vw[0], vw[1])
                if j == 0:
                    kb.wait("tensor", "rhs", kb.cur("rhs"))
                bA = (2 * j) % 4
                bB = (2 * j + 1) % 4
                banks.acquire(bA)
                banks.acquire(bB)
                m = None
                for c in range(CK):
                    t_mm(pb[bA][:, :], slot[:, c * 128:(c + 1) * 128],
                         ybuf[:, c * TL:c * TL + NF], c == 0, c == CK - 1)
                    m = t_mm(pb[bB][:, :], slot[:, c * 128:(c + 1) * 128],
                             ybuf[:, c * TL + NF:c * TL + 2 * NF],
                             c == 0, c == CK - 1, inc="mm" if c == CK - 1 else None) or m
                o_consume(j, 0, bA, m)
                o_consume(j, 1, bB, m)
                ringA_consumed()

        if dbg and stages == 2:
            kb.sem("std")
            kb.wait("gpsimd", "rhs", kb.cur("rhs"))
            kb.wait("gpsimd", "evS", kb.cur("evS"))
            dma("gpsimd", o_dbg[2][:, :], ybuf[:, :], "std")
            dma("gpsimd", o_dbg[0][:, 0:CK], kcol[:, :], "std")
            dma("gpsimd", o_dbg[0][:, CK:2 * CK], vcol[:, :], "std")
            dma("gpsimd", o_dbg[0][:, 2 * CK:3 * CK], rcol[:, :], "std")
            dma("gpsimd", o_dbg[0][:, 3 * CK:4 * CK], mixcol_b[:, :], "std")
            dma("gpsimd", o_dbg[0][:, 4 * CK:5 * CK], cB[:, :], "std")
            dma("gpsimd", o_dbg[1][:, :], xn[:, :], "std")
            dma("gpsimd", o_dbg[0][:, 5 * CK:6 * CK], ck_t[:, :], "std")
            dma("gpsimd", o_dbg[0][:, 6 * CK:7 * CK], cv_t[:, :], "std")
            dma("gpsimd", o_dbg[0][:, 7 * CK:8 * CK], cr_t[:, :], "std")
            dma("gpsimd", o_dbg[0][:, 128:256], vperm[:, 0:128], "std")
            kb.wait("gpsimd", "std", kb.cur("std"))
        out_matmuls()
        return kb.cur("mm")

    def emit_ffn(layer, v_xn, m_ws_free, stats_hook=None):
        sgs_o = VOFF[f"sgs{layer}"]
        sus_o = VOFF[f"sus{layer}"]
        sds_o = VOFF[f"sds{layer}"]
        wgL, wuL, wdL = wg[layer], wu[layer], wd[layer]

        kb.wait("tensor", "rhs", v_xn)
        banks.acquire(6)
        banks.acquire(7)
        m_cs = None
        for c in range(CK):
            t_mm(pb[6][0:1, :], ones_col_b[:, :], xn[:, c * TL:c * TL + NF],
                 c == 0, c == CK - 1)
            m_cs = t_mm(pb[7][0:1, :], ones_col_b[:, :], xn[:, c * TL + NF:c * TL + 2 * NF],
                        c == 0, c == CK - 1, inc="mm" if c == CK - 1 else None) or m_cs
        kb.wait("scalar", "mm", m_cs)
        s_cs = None
        for n in range(NT):
            if n == NT - 1:
                s_cs = kb.inc("scl")
                kb.op("scalar", lambda e, n=n: e.activation(
                    out=csrow[0:1, n * NF:(n + 1) * NF], in_=pb[6 + n][0:1, :],
                    func=AF.Copy, scale=-128.0).then_inc(kb.h["scl"], 1))
            else:
                kb.op("scalar", lambda e, n=n: e.activation(
                    out=csrow[0:1, n * NF:(n + 1) * NF], in_=pb[6 + n][0:1, :],
                    func=AF.Copy, scale=-128.0))
        banks.release(6, "scl", s_cs)
        banks.release(7, "scl", s_cs)
        kb.wait("tensor", "scl", s_cs)
        banks.acquire(4)
        banks.acquire(5)
        t_mm(pb[4][:, :], ones_row_b[:, :], csrow[0:1, 0:NF], True, True)
        m_csb = t_mm(pb[5][:, :], ones_row_b[:, :], csrow[0:1, NF:2 * NF], True, True, inc="mm")
        kb.wait("vector", "mm", m_csb)
        kb.op("vector", lambda e: e.tensor_copy(out=csb[:, 0:NF], in_=pb[4][:, :]))
        v_csb = v_op(lambda e: e.tensor_copy(out=csb[:, NF:2 * NF], in_=pb[5][:, :]), inc="evV")
        banks.release(4, "evV", v_csb)
        banks.release(5, "evV", v_csb)

        for g in range(NG):
            for ic in range(GI):
                j = g * GI + ic
                slot_g, vwg = ringA_load(wgL[j])
                slot_u, vwu = ringA_load(wuL[j])
                if ic == 0 and g == 0:
                    kb.wait("vector", "mm", m_ws_free)
                kb.wait("tensor", vwg[0], vwg[1])
                gbanks = [4 * (ic % 2), 4 * (ic % 2) + 1]
                banks.acquire(gbanks[0])
                banks.acquire(gbanks[1])
                m_g = None
                for c in range(CK):
                    t_mm(pb[gbanks[0]][:, :], slot_g[:, c * 128:(c + 1) * 128],
                         xn[:, c * TL:c * TL + NF], c == 0, c == CK - 1)
                    m_g = t_mm(pb[gbanks[1]][:, :], slot_g[:, c * 128:(c + 1) * 128],
                               xn[:, c * TL + NF:c * TL + 2 * NF], c == 0, c == CK - 1,
                               inc="mm" if c == CK - 1 else None) or m_g
                ringA_consumed()
                kb.wait("tensor", vwu[0], vwu[1])
                ubanks = [4 * (ic % 2) + 2, 4 * (ic % 2) + 3]
                banks.acquire(ubanks[0])
                banks.acquire(ubanks[1])
                m_u = None
                for c in range(CK):
                    t_mm(pb[ubanks[0]][:, :], slot_u[:, c * 128:(c + 1) * 128],
                         xn[:, c * TL:c * TL + NF], c == 0, c == CK - 1)
                    m_u = t_mm(pb[ubanks[1]][:, :], slot_u[:, c * 128:(c + 1) * 128],
                               xn[:, c * TL + NF:c * TL + 2 * NF], c == 0, c == CK - 1,
                               inc="mm" if c == CK - 1 else None) or m_u
                ringA_consumed()
                for n in range(NT):
                    gbn, ubn = gbanks[n], ubanks[n]
                    scrs = scr2[n]
                    nsl = slice(n * NF, (n + 1) * NF)
                    kb.wait("vector", "mm", m_g)
                    vs_g = v_op(lambda e, gbn=gbn, nsl=nsl: e.tensor_tensor(
                        out=pb[gbn][:, :], in0=pb[gbn][:, :], in1=csb[:, nsl], op=ALU.add),
                        inc="evV")
                    kb.wait("scalar", "mm", m_g)
                    kb.wait("scalar", "evV", max(vs_g, scr2_cred[n]))
                    sv = kb.inc("evS")
                    kb.op("scalar", lambda e, gbn=gbn, scrs=scrs, j=j: e.activation(
                        out=scrs[:, :], in_=pb[gbn][:, :], func=AF.Silu,
                        scale=vperm[:, sgs_o + j:sgs_o + j + 1]).then_inc(kb.h["evS"], 1))
                    banks.release(gbn, "evS", sv)
                    kb.wait("vector", "mm", m_u)
                    vs_u = v_op(lambda e, ubn=ubn, nsl=nsl: e.tensor_tensor(
                        out=pb[ubn][:, :], in0=pb[ubn][:, :], in1=csb[:, nsl], op=ALU.add),
                        inc="evV")
                    kb.wait("vector", "evS", sv)
                    kb.wait("vector", "evV", vs_u)
                    sl = slice(ic * TL + n * NF, ic * TL + (n + 1) * NF)
                    vv = v_op(lambda e, ubn=ubn, scrs=scrs, sl=sl, j=j: e.scalar_tensor_tensor(
                        out=hbuf[:, sl], in0=pb[ubn][:, :], scalar=vperm[:, sus_o + j:sus_o + j + 1],
                        in1=scrs[:, :], op0=ALU.mult, op1=ALU.mult), inc="evV")
                    scr2_cred[n] = vv
                    banks.release(ubn, "evV", vv)
            v_h = kb.cur("evV")

            banks.acquire(6)
            banks.acquire(7)
            kb.wait("tensor", "evV", v_h)
            m_hs = None
            for ic in range(GI):
                t_mm(pb[6][0:1, :], ones_col_b[:, :], hbuf[:, ic * TL:ic * TL + NF],
                     ic == 0, ic == GI - 1)
                m_hs = t_mm(pb[7][0:1, :], ones_col_b[:, :], hbuf[:, ic * TL + NF:ic * TL + 2 * NF],
                            ic == 0, ic == GI - 1, inc="mm" if ic == GI - 1 else None) or m_hs
            kb.wait("vector", "mm", m_hs)
            v_hsr = None
            for n in range(NT):
                v_ser(lambda e, n=n: e.tensor_scalar(
                    out=hs_hi[0:1, n * NF:(n + 1) * NF], in0=pb[6 + n][0:1, :],
                    scalar1=-128.0, scalar2=None, op0=ALU.mult))
                v_hsr = v_ser(lambda e, n=n: e.scalar_tensor_tensor(
                    out=hs_lo[0:1, n * NF:(n + 1) * NF], in0=pb[6 + n][0:1, :], scalar=-128.0,
                    in1=hs_hi[0:1, n * NF:(n + 1) * NF], op0=ALU.mult, op1=ALU.subtract),
                    inc="evV" if n == NT - 1 else None) or v_hsr
            banks.release(6, "evV", v_hsr)
            banks.release(7, "evV", v_hsr)
            kb.wait("tensor", "evV", v_hsr)

            for o in range(16):
                slot, vw = ringD_load(wdL[g, o])
                kb.wait("tensor", vw[0], vw[1])
                bA = (2 * o) % 4
                bB = (2 * o + 1) % 4
                banks.acquire(bA)
                banks.acquire(bB)
                for ic in range(GI):
                    t_mm(pb[bA][:, :], slot[:, ic * 128:(ic + 1) * 128],
                         hbuf[:, ic * TL:ic * TL + NF], ic == 0, False)
                    t_mm(pb[bB][:, :], slot[:, ic * 128:(ic + 1) * 128],
                         hbuf[:, ic * TL + NF:ic * TL + 2 * NF], ic == 0, False)
                t_mm(pb[bA][:, :], ones_row_b[:, :], hs_hi[0:1, 0:NF], False, False)
                t_mm(pb[bB][:, :], ones_row_b[:, :], hs_hi[0:1, NF:2 * NF], False, False)
                t_mm(pb[bA][:, :], ones_row_b[:, :], hs_lo[0:1, 0:NF], False, True)
                m = t_mm(pb[bB][:, :], ones_row_b[:, :], hs_lo[0:1, NF:2 * NF],
                         False, True, inc="mm")
                kb.wait("vector", "mm", m)
                for n, bnk in ((0, bA), (1, bB)):
                    sl = slice(o * TL + n * NF, o * TL + (n + 1) * NF)
                    vv = v_op(lambda e, bnk=bnk, sl=sl, o=o: e.scalar_tensor_tensor(
                        out=xT[:, sl], in0=pb[bnk][:, :], scalar=vperm[:, sds_o + o:sds_o + o + 1],
                        in1=xT[:, sl], op0=ALU.mult, op1=ALU.add), inc="evV")
                    banks.release(bnk, "evV", vv)
                if stats_hook is not None and g == NG - 1:
                    stats_hook(o)
                ringD_consumed()
        return kb.cur("mm")

    def emit_l1_attn(m_ws_free):
        ise = vcolv("iseven")
        usc = vcolv("usec")
        m_q = {}
        m_g = {}
        qb = {}
        gb = {}
        sig_done = {}
        slot_free = {}

        def emit_chunk_matmuls(i):
            slot, vw = ringA_load(w1_t[i])
            kb.wait("tensor", vw[0], vw[1])
            qbanks = [4 * (i % 2), 4 * (i % 2) + 1]
            banks.acquire(qbanks[0])
            banks.acquire(qbanks[1])
            for c in range(CK):
                t_mm(pb[qbanks[0]][:, :], slot[:, c * 128:(c + 1) * 128],
                     xn[:, c * TL:c * TL + NF], c == 0, c == CK - 1)
                m_q[i] = t_mm(pb[qbanks[1]][:, :], slot[:, c * 128:(c + 1) * 128],
                              xn[:, c * TL + NF:c * TL + 2 * NF], c == 0, c == CK - 1,
                              inc="mm" if c == CK - 1 else None) or m_q.get(i)
            qb[i] = qbanks
            ringA_consumed()
            slot, vw = ringA_load(w1_t[16 + i])
            kb.wait("tensor", vw[0], vw[1])
            gbanks = [4 * (i % 2) + 2, 4 * (i % 2) + 3]
            banks.acquire(gbanks[0])
            banks.acquire(gbanks[1])
            for c in range(CK):
                t_mm(pb[gbanks[0]][:, :], slot[:, c * 128:(c + 1) * 128],
                     xn[:, c * TL:c * TL + NF], c == 0, c == CK - 1)
                m_g[i] = t_mm(pb[gbanks[1]][:, :], slot[:, c * 128:(c + 1) * 128],
                              xn[:, c * TL + NF:c * TL + 2 * NF], c == 0, c == CK - 1,
                              inc="mm" if c == CK - 1 else None) or m_g.get(i)
            gb[i] = gbanks
            ringA_consumed()

        def emit_chunk_sig(i):
            s = i % 8
            gbanks = gb[i]
            kb.wait("scalar", "mm", m_g[i])
            if i >= 8:
                kb.wait("scalar", "evV", slot_free[s])
            sg = None
            for n in range(NT):
                if n == NT - 1:
                    sg = kb.inc("scl")
                    kb.op("scalar", lambda e, bnk=gbanks[n], s=s, n=n: e.activation(
                        out=slotg[:, s * TL + n * NF:s * TL + (n + 1) * NF],
                        in_=pb[bnk][:, :], func=AF.Sigmoid).then_inc(kb.h["scl"], 1))
                else:
                    kb.op("scalar", lambda e, bnk=gbanks[n], s=s, n=n: e.activation(
                        out=slotg[:, s * TL + n * NF:s * TL + (n + 1) * NF],
                        in_=pb[bnk][:, :], func=AF.Sigmoid))
            sig_done[i] = sg
            for n in range(NT):
                banks.release(gbanks[n], "scl", sg)

        def emit_chunk_evac(i):
            s = i % 8
            kb.wait("vector", "mm", m_q[i])
            vq = None
            for n in range(NT):
                vq = v_op(lambda e, bnk=qb[i][n], s=s, n=n: e.tensor_scalar(
                    out=slotq[:, s * TL + n * NF:s * TL + (n + 1) * NF], in0=pb[bnk][:, :],
                    scalar1=1e-20, scalar2=None, op0=ALU.max),
                    inc="evV" if n == NT - 1 else None) or vq
            for n in range(NT):
                banks.release(qb[i][n], "evV", vq)
            kb.wait("vector", "scl", sig_done[i])
            kb.op("vector", lambda e, s=s: e.tensor_scalar(
                out=slotg[:, s * TL:(s + 1) * TL], in0=slotg[:, s * TL:(s + 1) * TL],
                scalar1=1e-20, scalar2=None, op0=ALU.max))
            v_ser(lambda e, s=s: e.tensor_tensor_scan(
                out=scanscr[:, :], data0=slotg[:, s * TL:(s + 1) * TL],
                data1=slotq[:, s * TL:(s + 1) * TL], initial=0.0,
                op0=ALU.mult, op1=ALU.add))
            v_ser(lambda e, i=i: e.tensor_copy(
                out=lastcols[:, i:i + 1], in_=scanscr[:, TL - 1:TL]))

        def emit_scan2(i, cbf):
            s = i % 8
            i8 = i % 8
            v_ser(lambda e, s=s, cbf=cbf, i8=i8: e.tensor_tensor_scan(
                out=scanscr2[:, :], data0=slotg[:, s * TL:(s + 1) * TL],
                data1=slotq[:, s * TL:(s + 1) * TL], initial=cbf[:, i8:i8 + 1],
                op0=ALU.mult, op1=ALU.add))
            v_fin = v_op(lambda e, i=i: e.tensor_tensor(
                out=xT[:, i * TL:(i + 1) * TL], in0=xT[:, i * TL:(i + 1) * TL],
                in1=scanscr2[:, :], op=ALU.add), inc="evV")
            slot_free[s] = v_fin
            kb.wait("sync", "evV", v_fin)
            dma("sync", o_x[:, i * TL:(i + 1) * TL], xT[:, i * TL:(i + 1) * TL], "st")

        kb.wait("vector", "mm", m_ws_free)
        for i in range(8):
            emit_chunk_matmuls(i)
            emit_chunk_sig(i)
            emit_chunk_evac(i)
        v_cc1 = v_ser(lambda e: e.tensor_scalar(
            out=ccsb1[:, :], in0=lastcols[:, 0:8], scalar1=ise, scalar2=None, op0=ALU.mult),
            inc="vcc")
        emit_chunk_matmuls(8)
        emit_chunk_matmuls(9)
        kb.wait("gpsimd", "vcc", v_cc1)
        vd = dma("gpsimd", cc_in1[:, :], ccsb1[:, :], "wq")
        kb.wait("gpsimd", "wq", vd)
        c1 = kb.inc("ccs")
        kb.op("gpsimd", lambda e: e.collective_compute(
            "AllReduce", ALU.add, replica_groups=[[0, 1], [2, 3], [4, 5], [6, 7]],
            ins=[cc_in1.ap()], outs=[cc_out1.ap()]).then_inc(kb.h["ccs"], 1))
        kb.wait("gpsimd", "ccs", c1)
        v_r1 = dma("gpsimd", ccres1[:, :], cc_out1[:, :], "wq")
        kb.wait("vector", "wq", v_r1)
        v_ser(lambda e: e.tensor_scalar(
            out=cbuf1[:, :], in0=ccres1[:, :], scalar1=usc, scalar2=None, op0=ALU.mult))
        for k in range(8):
            emit_scan2(k, cbuf1)
            emit_chunk_sig(8 + k)
            emit_chunk_evac(8 + k)
            if 10 + k <= 15:
                emit_chunk_matmuls(10 + k)
        v_cc2 = v_ser(lambda e: e.tensor_scalar(
            out=ccsb2[:, :], in0=lastcols[:, 8:16], scalar1=ise, scalar2=None, op0=ALU.mult),
            inc="vcc")
        kb.wait("gpsimd", "vcc", v_cc2)
        vd = dma("gpsimd", cc_in2[:, :], ccsb2[:, :], "wq")
        kb.wait("gpsimd", "wq", vd)
        c2 = kb.inc("ccs")
        kb.op("gpsimd", lambda e: e.collective_compute(
            "AllReduce", ALU.add, replica_groups=[[0, 1], [2, 3], [4, 5], [6, 7]],
            ins=[cc_in2.ap()], outs=[cc_out2.ap()]).then_inc(kb.h["ccs"], 1))
        kb.wait("gpsimd", "ccs", c2)
        v_r2 = dma("gpsimd", ccres2[:, :], cc_out2[:, :], "wq")
        kb.wait("vector", "wq", v_r2)
        v_ser(lambda e: e.tensor_scalar(
            out=cbuf2[:, :], in0=ccres2[:, :], scalar1=usc, scalar2=None, op0=ALU.mult))
        for k in range(8):
            emit_scan2(8 + k, cbuf2)

    def emit_dbg_store(idx):
        kb.wait("sync", "evV", kb.cur("evV"))
        kb.wait("sync", "rhs", kb.cur("rhs"))
        for q in range(4):
            dma("sync", o_dbg[idx][:, q * 4096:(q + 1) * 4096], xT[:, q * 4096:(q + 1) * 4096], "st")
        kb.wait("vector", "st", kb.cur("st"))

    if stages >= 1:
        v_xn0 = emit_layernorm(0)
    if stages >= 2:
        m_free0 = emit_l0_attn()
        if dbg and stages != 2:
            emit_dbg_store(0)
    if stages >= 3:
        m_ffn0 = emit_ffn(0, v_xn0, m_free0,
                          stats_hook=ln_stats_hook if stages >= 4 else None)
        if dbg:
            emit_dbg_store(1)
    if stages >= 4:
        kb.wait("vector", "mm", m_ffn0)
        v_xn1 = emit_layernorm(1, fused=True)
    if stages >= 5:
        m_ffn1 = emit_ffn(1, v_xn1, kb.cur("mm"))
        if dbg:
            emit_dbg_store(2)
    if stages >= 6:
        emit_l1_attn(m_ffn1)
    else:
        kb.wait("sync", "evV", kb.cur("evV"))
        kb.wait("sync", "rhs", kb.cur("rhs"))
        kb.wait("sync", "evS", kb.cur("evS"))
        kb.wait("sync", "mm", kb.cur("mm"))
        for q in range(4):
            dma("sync", o_x[:, q * 4096:(q + 1) * 4096], xT[:, q * 4096:(q + 1) * 4096], "st")

    kb.wait("sync", "st", kb.cur("st"))
    kb.wait("gpsimd", "st", kb.cur("st"))

    with contextlib.ExitStack() as es:
        for s in kb.cnt:
            kb.h[s] = es.enter_context(nc.semaphore(s))
        block = es.enter_context(nc.Block())
        for eng in ENGINES:
            ops = kb.ops[eng]

            def runner(e, ops=ops):
                for f in ops:
                    f(e)
            getattr(block, eng)(runner)

    return nc


def _host_layout(inputs):
    x = np.asarray(inputs["L0_attn_x"])

    def tile_w(w, oc):
        return np.ascontiguousarray(
            np.asarray(w).reshape(oc, 128, 16, 128).transpose(0, 3, 2, 1)).reshape(oc, 128, 2048)

    wk = tile_w(inputs["L0_attn_w_key"], 16)
    wv = tile_w(inputs["L0_attn_w_value"], 16)
    wr = tile_w(inputs["L0_attn_w_rec"], 16)
    wo = tile_w(inputs["L0_attn_w_out"], 16)
    w1 = tile_w(inputs["L1_attn_W"], 32)

    def tile_d(w):
        return np.ascontiguousarray(
            np.asarray(w).reshape(16, 128, NG, GI, 128).transpose(2, 0, 4, 3, 1)).reshape(NG, 16, 128, GI * 128)

    wgs = [tile_w(inputs["L0_ffn_gate_w"], 64), tile_w(inputs["L1_ffn_gate_w"], 64)]
    wus = [tile_w(inputs["L0_ffn_up_w"], 64), tile_w(inputs["L1_ffn_up_w"], 64)]
    wds = [tile_d(inputs["L0_ffn_down_w"]), tile_d(inputs["L1_ffn_down_w"])]

    def v16(a):
        return np.asarray(a).reshape(16, 128).T

    base = np.zeros((128, VW), dtype=np.float32)
    base[:, VOFF["tf"]:VOFF["tf"] + 16] = v16(inputs["L0_attn_time_first"])
    base[:, VOFF["tmk"]:VOFF["tmk"] + 16] = v16(inputs["L0_attn_time_mix_k"])
    base[:, VOFF["tmv"]:VOFF["tmv"] + 16] = v16(inputs["L0_attn_time_mix_v"])
    base[:, VOFF["tmr"]:VOFF["tmr"] + 16] = v16(inputs["L0_attn_time_mix_r"])
    base[:, VOFF["xx"]:VOFF["xx"] + 16] = v16(inputs["L0_attn_xx"])
    for l in range(2):
        base[:, VOFF[f"sgs{l}"]:VOFF[f"sgs{l}"] + 64] = np.asarray(inputs[f"L{l}_ffn_gate_scale"]).reshape(64, 128).T
        base[:, VOFF[f"sus{l}"]:VOFF[f"sus{l}"] + 64] = np.asarray(inputs[f"L{l}_ffn_up_scale"]).reshape(64, 128).T
        base[:, VOFF[f"sds{l}"]:VOFF[f"sds{l}"] + 16] = v16(inputs[f"L{l}_ffn_down_scale"])

    in_maps = []
    for c in range(NCORES):
        b, h = c // 2, c % 2
        xs = np.ascontiguousarray(
            np.asarray(x[b, h * TL:(h + 1) * TL, :]).reshape(TL, 16, 128).transpose(2, 1, 0)).reshape(128, 16 * TL)
        v = base.copy()
        if h == 0:
            v[:, VOFF["aa"]:VOFF["aa"] + 16] = v16(inputs["L0_attn_aa"])
            v[:, VOFF["bb"]:VOFF["bb"] + 16] = v16(inputs["L0_attn_bb"])
            v[:, VOFF["pp"]:VOFF["pp"] + 16] = v16(inputs["L0_attn_pp"])
            v[:, VOFF["iseven"]] = 1.0
            v[:, VOFF["usec"]] = 0.0
        else:
            v[:, VOFF["iseven"]] = 0.0
            v[:, VOFF["usec"]] = 1.0
        in_maps.append(dict(
            x_in=xs, vecs=v, wk_t=wk, wv_t=wv, wr_t=wr, wo_t=wo, w1_t=w1,
            wg0=wgs[0], wu0=wus[0], wd0=wds[0],
            wg1=wgs[1], wu1=wus[1], wd1=wds[1]))
    return in_maps


def kernel(**inputs):
    in_maps = _host_layout(inputs)
    nc = build_program(dbg=False)
    res = run_bass_kernel_spmd(nc, in_maps, core_ids=list(range(NCORES)))
    out = np.empty((B, T, C), dtype=np.float32)
    for c in range(NCORES):
        b, h = c // 2, c % 2
        arr = np.asarray(res.results[c]["o_x"]).reshape(128, 16, TL).transpose(2, 1, 0).reshape(TL, C)
        out[b, h * TL:(h + 1) * TL, :] = arr
    return out


# version 06
# speedup vs baseline: 1.1947x; 1.1947x over previous
import contextlib
import numpy as np
import concourse.bass as bass
import concourse.mybir as mybir
from concourse.bass_utils import run_bass_kernel_spmd

F32 = mybir.dt.float32
BF16 = mybir.dt.bfloat16
I32 = mybir.dt.int32
AF = mybir.ActivationFunctionType
ALU = mybir.AluOpType

NCORES = 8
B, T, C, I = 4, 2048, 2048, 8192
TL = T // 2
CK = C // 128
IK = I // 128
GI = 32
NG = IK // GI
NF = 512
NT = TL // NF
EPS = 1e-5

VOFF = dict(tf=0, tmk=16, tmv=32, tmr=48, xx=64, aa=80, bb=96, pp=112,
            iseven=128, usec=129,
            sgs0=132, sus0=196, sds0=260, sgs1=276, sus1=340, sds1=404, end=420)
VW = 420

ENGINES = ("sync", "scalar", "vector", "tensor", "gpsimd")


class KB:

    def __init__(self):
        self.ops = {e: [] for e in ENGINES}
        self.cnt = {}
        self.h = {}

    def sem(self, name):
        self.cnt.setdefault(name, 0)

    def inc(self, name, n=1):
        self.cnt[name] += n
        return self.cnt[name]

    def cur(self, name):
        return self.cnt[name]

    def op(self, eng, f):
        self.ops[eng].append(f)

    def wait(self, eng, sname, val):
        if val <= 0:
            return
        self.op(eng, lambda e, s=sname, v=val: e.wait_ge(self.h[s], v))


class Banks:
    def __init__(self, kb, handles):
        self.kb = kb
        self.pb = handles
        self.cond = [[] for _ in handles]

    def acquire(self, b):
        for (s, v) in self.cond[b]:
            self.kb.wait("tensor", s, v)
        self.cond[b] = []

    def release(self, b, sname, val):
        self.cond[b].append((sname, val))


def build_program(dbg=False, stages=6, lnsub=5, attnsub=6, attnj=CK):
    nc = bass.Bass("TRN2", target_bir_lowering=False)

    x_in = nc.dram_tensor("x_in", [128, CK * TL], F32, kind="ExternalInput").ap()
    vecs = nc.dram_tensor("vecs", [128, VW], F32, kind="ExternalInput").ap()
    wk_t = nc.dram_tensor("wk_t", [CK, 128, 2048], F32, kind="ExternalInput").ap()
    wv_t = nc.dram_tensor("wv_t", [CK, 128, 2048], F32, kind="ExternalInput").ap()
    wr_t = nc.dram_tensor("wr_t", [CK, 128, 2048], F32, kind="ExternalInput").ap()
    wo_t = nc.dram_tensor("wo_t", [CK, 128, 2048], F32, kind="ExternalInput").ap()
    w1_t = nc.dram_tensor("w1_t", [2 * CK, 128, 2048], F32, kind="ExternalInput").ap()
    wg = [nc.dram_tensor(f"wg{l}", [IK, 128, 2048], I32, kind="ExternalInput").ap() for l in range(2)]
    wu = [nc.dram_tensor(f"wu{l}", [IK, 128, 2048], I32, kind="ExternalInput").ap() for l in range(2)]
    wd = [nc.dram_tensor(f"wd{l}", [NG, 16, 128, GI * 128], I32, kind="ExternalInput").ap() for l in range(2)]
    o_x = nc.dram_tensor("o_x", [128, CK * TL], F32, kind="ExternalOutput").ap()
    o_dbg = None
    if dbg:
        o_dbg = nc.dram_tensor("o_dbg", [3, 128, CK * TL], F32, kind="ExternalOutput").ap()

    cc_in1 = nc.dram_tensor("cc_in1", [128, 8], F32)
    cc_out1 = nc.dram_tensor("cc_out1", [128, 8], F32)
    cc_in2 = nc.dram_tensor("cc_in2", [128, 8], F32)
    cc_out2 = nc.dram_tensor("cc_out2", [128, 8], F32)

    BASE = 16512
    A = lambda name, shape, dtype, off: nc.alloc_sbuf_tensor_at(name, shape, dtype, offset=BASE + off)

    xT = A("xT", [128, CK * TL], F32, 0)
    xn = A("xn", [128, CK * TL], BF16, 65536)
    WS = 98304
    acc = A("acc", [128, TL], F32, WS + 0)
    acc2 = A("acc2", [128, TL], F32, WS + 4096)
    sqtmp = A("sqtmp", [128, TL], F32, WS + 8192)
    lnscr = A("lnscr", [128, TL], F32, WS + 12288)
    mrow = A("mrow", [1, TL], F32, WS + 16384)
    rowA = A("rowA", [1, TL], F32, WS + 20480)
    rowB = A("rowB", [1, TL], F32, WS + 24576)
    mixbuf = A("mixbuf", [128, CK * TL], BF16, WS + 0)
    ybuf = A("ybuf", [128, CK * TL], BF16, WS + 32768)
    hbuf = A("hbuf", [128, GI * TL], BF16, WS + 0)
    slotq = A("slotq", [128, 8 * TL], F32, WS + 0)
    slotg = A("slotg", [128, 8 * TL], F32, WS + 32768)

    STG = 163840
    ringA = [A(f"ringA{i}", [128, 2048], BF16, STG + i * 4096) for i in range(3)]
    ringD = [A(f"ringD{i}", [128, 4096], BF16, STG + 12288 + i * 8192) for i in range(2)]
    scanscr = A("scanscr", [128, TL], F32, STG + 12288)
    scanscr2 = A("scanscr2", [128, TL], F32, STG + 16384)
    csb = A("csb", [128, TL], BF16, STG + 28672)
    brb = A("brb", [128, TL], BF16, STG + 30720)

    MISC = 196608
    scr2 = [A(f"scr2_{i}", [128, NF], F32, MISC + i * 2048) for i in range(2)]
    vperm = A("vperm", [128, VW], F32, MISC + 4096)
    mo = MISC + 4096 + 1696

    def MA(name, shape, dtype, sz):
        nonlocal mo
        t = A(name, shape, dtype, mo)
        mo += sz
        return t
    ck_t = MA("ck_t", [128, CK], F32, 64)
    cv_t = MA("cv_t", [128, CK], F32, 64)
    cr_t = MA("cr_t", [128, CK], F32, 64)
    kcol = MA("kcol", [128, CK], F32, 64)
    vcol = MA("vcol", [128, CK], F32, 64)
    rcol = MA("rcol", [128, CK], F32, 64)
    cA = MA("cA", [128, CK], F32, 64)
    cB = MA("cB", [128, CK], F32, 64)
    cC = MA("cC", [128, CK], F32, 64)
    cD = MA("cD", [128, CK], F32, 64)
    lastcols = MA("lastcols", [128, CK], F32, 64)
    mixcol_b = MA("mixcol_b", [128, CK], BF16, 32)
    ccsb1 = MA("ccsb1", [128, 8], F32, 32)
    ccsb2 = MA("ccsb2", [128, 8], F32, 32)
    ccres1 = MA("ccres1", [128, 8], F32, 32)
    ccres2 = MA("ccres2", [128, 8], F32, 32)
    cbuf1 = MA("cbuf1", [128, 8], F32, 32)
    cbuf2 = MA("cbuf2", [128, 8], F32, 32)
    epsc = MA("epsc", [128, 1], F32, 32)
    ones_col_f = MA("ones_col_f", [128, 1], F32, 32)
    ones_col_b = MA("ones_col_b", [128, 1], BF16, 32)
    ones_row_f = MA("ones_row_f", [1, 128], F32, 512)
    ones_row_b = MA("ones_row_b", [1, 128], BF16, 256)
    csrow = MA("csrow", [1, TL], BF16, 2048)
    hs_hi = MA("hs_hi", [1, TL], BF16, 2048)
    hs_lo = MA("hs_lo", [1, TL], BF16, 2048)
    assert mo - MISC <= 229344 - BASE - MISC, mo

    def vcolv(name, j=0):
        off = VOFF[name]
        return vperm[:, off + j:off + j + 1]

    pb = [nc.alloc_psum_tensor(f"pb{i}", [128, NF], F32) for i in range(8)]

    kb = KB()
    for s in ("xl", "wq", "mm", "evS", "evV", "rhs", "scl", "vcc", "ccs", "st",
              "vsn", "gsn", "wqA0", "wqA1", "wqA2", "wqD0", "wqD1"):
        kb.sem(s)
    banks = Banks(kb, pb)
    scr2_cred = [0, 0]

    def dma(eng, out_ap, in_ap, sname):
        v = kb.inc(sname, 16)
        kb.op(eng, lambda e, o=out_ap, i=in_ap, s=sname: e.dma_start(out=o, in_=i).then_inc(kb.h[s], 16))
        return v

    def t_mm(out, lhsT, rhs, start, stop, inc=None):
        if inc is None:
            kb.op("tensor", lambda e, o=out, l=lhsT, r=rhs, s=start, p=stop:
                  e.matmul(o, l, r, start=s, stop=p))
            return None
        v = kb.inc(inc)
        kb.op("tensor", lambda e, o=out, l=lhsT, r=rhs, s=start, p=stop, sn=inc:
              e.matmul(o, l, r, start=s, stop=p).then_inc(kb.h[sn], 1))
        return v

    v_edge = [None]

    def v_ser(f, inc=None):
        if v_edge[0] is not None:
            kb.wait("vector", v_edge[0][0], v_edge[0][1])
        s = inc or "vsn"
        v = kb.inc(s)
        kb.op("vector", lambda e, f=f, s=s: f(e).then_inc(kb.h[s], 1))
        v_edge[0] = (s, v)
        return v

    def v_op(f, inc=None):
        if inc is None:
            kb.op("vector", f)
            return None
        v = kb.inc(inc)
        kb.op("vector", lambda e, f=f, s=inc: f(e).then_inc(kb.h[s], 1))
        return v

    ringA_loads = [0]
    ringA_marks = []

    def ringA_load(src_ap):
        n = ringA_loads[0]
        slot = ringA[n % 3]
        sname = f"wqA{n % 3}"
        if n >= 3:
            kb.wait("gpsimd", "mm", ringA_marks[n - 3])
        v = dma("gpsimd", slot[:, :], src_ap, sname)
        ringA_loads[0] += 1
        return slot, (sname, v)

    def ringA_consumed():
        ringA_marks.append(kb.cur("mm"))

    ringD_loads = [0]
    ringD_marks = []

    def ringD_load(src_ap):
        n = ringD_loads[0]
        slot = ringD[n % 2]
        sname = f"wqD{n % 2}"
        if n >= 2:
            kb.wait("gpsimd", "mm", ringD_marks[n - 2])
        v = dma("gpsimd", slot[:, :], src_ap, sname)
        ringD_loads[0] += 1
        return slot, (sname, v)

    def ringD_consumed():
        ringD_marks.append(kb.cur("mm"))

    for q in range(4):
        dma("sync", xT[:, q * 4096:(q + 1) * 4096], x_in[:, q * 4096:(q + 1) * 4096], "xl")
    v_vecs = dma("sync", vperm[:, :], vecs, "xl")

    kb.op("vector", lambda e: e.memset(epsc[:, :], EPS))
    kb.op("vector", lambda e: e.memset(ones_col_f[:, :], 1.0))
    kb.op("vector", lambda e: e.memset(ones_col_b[:, :], 1.0))
    kb.op("vector", lambda e: e.memset(ones_row_f[:, :], 1.0))
    kb.op("vector", lambda e: e.memset(ones_row_b[:, :], 1.0))
    kb.wait("vector", "xl", v_vecs)
    for nm, dst in (("tmk", ck_t), ("tmv", cv_t), ("tmr", cr_t)):
        o = VOFF[nm]
        xxs = vperm[:, VOFF["xx"]:VOFF["xx"] + CK]
        v_ser(lambda e, o=o, xxs=xxs: e.tensor_tensor(
            out=cD[:, :], in0=xxs, in1=vperm[:, o:o + CK], op=ALU.mult))
        v_ser(lambda e, d=dst, xxs=xxs: e.tensor_tensor(
            out=d[:, :], in0=xxs, in1=cD[:, :], op=ALU.subtract))
    v_xloaded = kb.cur("xl")

    def ln_stats_hook(o):
        if o == 0:
            for b in range(4, 8):
                for (s, v) in banks.cond[b]:
                    kb.wait("vector", s, v)
                banks.cond[b] = []
        for n in range(NT):
            xsl = xT[:, o * TL + n * NF:o * TL + (n + 1) * NF]
            if o == 0:
                kb.op("vector", lambda e, n=n, xsl=xsl: e.tensor_scalar(
                    out=pb[4 + n][:, :], in0=xsl, scalar1=1.0, scalar2=None, op0=ALU.mult))
            else:
                kb.op("vector", lambda e, n=n, xsl=xsl: e.tensor_tensor(
                    out=pb[4 + n][:, :], in0=pb[4 + n][:, :], in1=xsl, op=ALU.add))
            kb.op("vector", lambda e, n=n, xsl=xsl: e.tensor_tensor(
                out=scr2[n][:, :], in0=xsl, in1=xsl, op=ALU.mult))
            if o == 0:
                kb.op("vector", lambda e, n=n: e.tensor_scalar(
                    out=pb[6 + n][:, :], in0=scr2[n][:, :], scalar1=1.0, scalar2=None, op0=ALU.mult))
            else:
                kb.op("vector", lambda e, n=n: e.tensor_tensor(
                    out=pb[6 + n][:, :], in0=pb[6 + n][:, :], in1=scr2[n][:, :], op=ALU.add))
        if o == 15:
            vv = v_op(lambda e: e.tensor_copy(out=lnscr[:, 0:1], in_=pb[4][:, 0:1]), inc="evV")
            scr2_cred[0] = vv
            scr2_cred[1] = vv

    def emit_layernorm(layer, fused=False):
        if layer == 0:
            kb.wait("vector", "xl", v_xloaded)
            kb.wait("gpsimd", "xl", v_xloaded)
        if fused:
            for n in range(NT):
                kb.op("vector", lambda e, n=n: e.tensor_copy(
                    out=acc[:, n * NF:(n + 1) * NF], in_=pb[4 + n][:, :]))
            v_acc = None
            for n in range(NT):
                v_acc = v_op(lambda e, n=n: e.tensor_copy(
                    out=acc2[:, n * NF:(n + 1) * NF], in_=pb[6 + n][:, :]),
                    inc="rhs" if n == NT - 1 else None) or v_acc
            for b in range(4, 8):
                banks.release(b, "rhs", v_acc)
            kb.wait("tensor", "rhs", v_acc)
            for bnk in range(4):
                banks.acquire(bnk)
            for n in range(NT):
                t_mm(pb[n][0:1, :], ones_col_f[:, :], acc[:, n * NF:(n + 1) * NF], True, True)
            m_stats = None
            for n in range(NT):
                m_stats = t_mm(pb[2 + n][0:1, :], ones_col_f[:, :], acc2[:, n * NF:(n + 1) * NF],
                               True, True, inc="mm" if n == NT - 1 else None) or m_stats
            return _ln_tail(m_stats)
        kb.op("gpsimd", lambda e: e.tensor_tensor(out=acc[:, :], in0=xT[:, 0:TL], in1=xT[:, TL:2 * TL], op=ALU.add))
        g_acc = None
        for k in range(2, CK):
            g_acc = kb.inc("gsn") if k == CK - 1 else None
            if k == CK - 1:
                kb.op("gpsimd", lambda e, k=k: e.tensor_tensor(
                    out=acc[:, :], in0=acc[:, :], in1=xT[:, k * TL:(k + 1) * TL], op=ALU.add).then_inc(kb.h["gsn"], 1))
            else:
                kb.op("gpsimd", lambda e, k=k: e.tensor_tensor(
                    out=acc[:, :], in0=acc[:, :], in1=xT[:, k * TL:(k + 1) * TL], op=ALU.add))
        kb.op("vector", lambda e: e.tensor_tensor(out=acc2[:, :], in0=xT[:, 0:TL], in1=xT[:, 0:TL], op=ALU.mult))
        v_acc = None
        for k in range(1, CK):
            kb.op("vector", lambda e, k=k: e.tensor_tensor(
                out=sqtmp[:, :], in0=xT[:, k * TL:(k + 1) * TL], in1=xT[:, k * TL:(k + 1) * TL], op=ALU.mult))
            v_acc = v_op(lambda e: e.tensor_tensor(out=acc2[:, :], in0=acc2[:, :], in1=sqtmp[:, :], op=ALU.add),
                         inc="rhs" if k == CK - 1 else None) or v_acc
        kb.wait("tensor", "gsn", g_acc)
        if lnsub < 2:
            return v_acc

        kb.wait("tensor", "rhs", v_acc)
        for bnk in range(4):
            banks.acquire(bnk)
        for n in range(NT):
            t_mm(pb[n][0:1, :], ones_col_f[:, :], acc[:, n * NF:(n + 1) * NF], True, True)
        m_stats = None
        for n in range(NT):
            m_stats = t_mm(pb[2 + n][0:1, :], ones_col_f[:, :], acc2[:, n * NF:(n + 1) * NF],
                           True, True, inc="mm" if n == NT - 1 else None) or m_stats
        if lnsub < 3:
            for bnk in range(4):
                banks.release(bnk, "mm", m_stats)
            return v_acc

        return _ln_tail(m_stats)

    def _ln_tail(m_stats):
        kb.wait("vector", "mm", m_stats)
        for n in range(NT):
            kb.op("vector", lambda e, n=n: e.tensor_scalar(
                out=mrow[0:1, n * NF:(n + 1) * NF], in0=pb[n][0:1, :], scalar1=1.0 / C, scalar2=None, op0=ALU.mult))
            kb.op("vector", lambda e, n=n: e.tensor_scalar(
                out=rowA[0:1, n * NF:(n + 1) * NF], in0=pb[2 + n][0:1, :], scalar1=1.0 / C, scalar2=None, op0=ALU.mult))
        kb.op("vector", lambda e: e.tensor_tensor(out=rowB[0:1, :], in0=mrow[0:1, :], in1=mrow[0:1, :], op=ALU.mult))
        v_var = v_op(lambda e: e.tensor_tensor(out=rowA[0:1, :], in0=rowA[0:1, :], in1=rowB[0:1, :], op=ALU.subtract),
                     inc="rhs")

        kb.wait("scalar", "rhs", v_var)
        s_sqrt = kb.inc("scl")
        kb.op("scalar", lambda e: e.activation(
            out=rowB[0:1, :], in_=rowA[0:1, :], func=AF.Sqrt, bias=epsc[0:1, :]).then_inc(kb.h["scl"], 1))
        kb.wait("vector", "scl", s_sqrt)
        v_rstd = v_op(lambda e: e.reciprocal(out=rowA[0:1, :], in_=rowB[0:1, :]), inc="rhs")
        if lnsub < 4:
            for bnk in range(4):
                banks.release(bnk, "rhs", v_rstd)
            return v_rstd

        kb.wait("tensor", "rhs", v_rstd)
        for bnk in range(4, 8):
            banks.acquire(bnk)
        for n in range(NT):
            t_mm(pb[4 + n][:, :], ones_row_f[:, :], mrow[0:1, n * NF:(n + 1) * NF], True, True)
        m_bc = None
        for n in range(NT):
            m_bc = t_mm(pb[6 + n][:, :], ones_row_f[:, :], rowA[0:1, n * NF:(n + 1) * NF],
                        True, True, inc="mm" if n == NT - 1 else None) or m_bc
        if lnsub < 5:
            for bnk in range(8):
                banks.release(bnk, "mm", m_bc)
            return v_rstd

        kb.wait("vector", "mm", m_bc)
        for n in range(NT):
            kb.op("vector", lambda e, n=n: e.tensor_copy(
                out=csb[:, n * NF:(n + 1) * NF], in_=pb[4 + n][:, :]))
        v_bmb = None
        for n in range(NT):
            v_bmb = v_op(lambda e, n=n: e.tensor_copy(
                out=brb[:, n * NF:(n + 1) * NF], in_=pb[6 + n][:, :]),
                inc="evV" if n == NT - 1 else None) or v_bmb
        kb.wait("gpsimd", "evV", v_bmb)
        g_xn = None
        for k in range(10, CK):
            kb.op("gpsimd", lambda e, k=k: e.tensor_tensor(
                out=scanscr[:, :], in0=xT[:, k * TL:(k + 1) * TL], in1=csb[:, :], op=ALU.subtract))
            g_xn = kb.inc("gsn") if k == CK - 1 else None
            if k == CK - 1:
                kb.op("gpsimd", lambda e, k=k: e.tensor_tensor(
                    out=xn[:, k * TL:(k + 1) * TL], in0=scanscr[:, :], in1=brb[:, :],
                    op=ALU.mult).then_inc(kb.h["gsn"], 1))
            else:
                kb.op("gpsimd", lambda e, k=k: e.tensor_tensor(
                    out=xn[:, k * TL:(k + 1) * TL], in0=scanscr[:, :], in1=brb[:, :], op=ALU.mult))
        v_xn = None
        for k in range(10):
            for n in range(NT):
                sl = slice(k * TL + n * NF, k * TL + (n + 1) * NF)
                kb.op("vector", lambda e, sl=sl, n=n: e.tensor_tensor(
                    out=lnscr[:, 0:NF], in0=xT[:, sl], in1=pb[4 + n][:, :], op=ALU.subtract))
                v_xn = v_op(lambda e, sl=sl, n=n: e.tensor_tensor(
                    out=xn[:, sl], in0=lnscr[:, 0:NF], in1=pb[6 + n][:, :], op=ALU.mult),
                    inc="rhs" if (k == 9 and n == NT - 1) else None) or v_xn
        kb.wait("vector", "gsn", g_xn)
        kb.wait("tensor", "gsn", g_xn)
        for bnk in range(8):
            banks.release(bnk, "rhs", v_xn)
        return v_xn

    def emit_l0_attn():
        def emit_mix(tmname, coff):
            for k in range(CK):
                r = v_op(lambda e, k=k: e.tensor_scalar(
                    out=mixbuf[:, k * TL:(k + 1) * TL], in0=xn[:, k * TL:(k + 1) * TL],
                    scalar1=vcolv(tmname, k), scalar2=coff[:, k:k + 1], op0=ALU.mult, op1=ALU.add),
                    inc="rhs" if k == CK - 1 else None)
            return r

        for c in range(CK):
            v_op(lambda e, c=c: e.tensor_scalar(
                out=mixcol_b[:, c:c + 1], in0=xn[:, c * TL:c * TL + 1],
                scalar1=vcolv("tmk", c), scalar2=ck_t[:, c:c + 1], op0=ALU.mult, op1=ALU.add),
                inc="rhs" if c == CK - 1 else None)
        v_mix = kb.cur("rhs")

        def phase_matmuls(w_dram, consume, pre=None):
            for j in range(attnj):
                if pre is not None:
                    pre(j)
                slot, vw = ringA_load(w_dram[j])
                kb.wait("tensor", vw[0], vw[1])
                if j == 0:
                    kb.wait("tensor", "rhs", kb.cur("rhs"))
                bA = (2 * j) % 4
                bB = (2 * j + 1) % 4
                banks.acquire(bA)
                banks.acquire(bB)
                m = None
                for c in range(CK):
                    t_mm(pb[bA][:, :], slot[:, c * 128:(c + 1) * 128],
                         mixbuf[:, c * TL:c * TL + NF], c == 0, c == CK - 1)
                    m = t_mm(pb[bB][:, :], slot[:, c * 128:(c + 1) * 128],
                             mixbuf[:, c * TL + NF:c * TL + 2 * NF],
                             c == 0, c == CK - 1, inc="mm" if c == CK - 1 else None) or m
                consume(j, 0, bA, m)
                consume(j, 1, bB, m)
                ringA_consumed()

        if attnsub < 3:
            return kb.cur("mm")

        emit_mix("tmv", cv_t)

        def kcol_step(j):
            slot, vw = ringA_load(wk_t[j])
            kb.wait("tensor", vw[0], vw[1])
            if j == 0:
                kb.wait("tensor", "rhs", v_mix)
            bnk = 4 + j % 2
            banks.acquire(bnk)
            m = None
            for c in range(CK):
                m = t_mm(pb[bnk][:, 0:1], slot[:, c * 128:(c + 1) * 128],
                         mixcol_b[:, c:c + 1], c == 0, c == CK - 1,
                         inc="mm" if c == CK - 1 else None) or m
            kb.wait("vector", "mm", m)
            vv = v_op(lambda e, bnk=bnk, j=j: e.tensor_copy(
                out=kcol[:, j:j + 1], in_=pb[bnk][:, 0:1]), inc="evV")
            banks.release(bnk, "evV", vv)
            ringA_consumed()

        def v_consume(j, n, bnk, m):
            if n == 0:
                kb.wait("vector", "mm", m)
                vv = v_op(lambda e, bnk=bnk, j=j: e.tensor_copy(
                    out=vcol[:, j:j + 1], in_=pb[bnk][:, 0:1]), inc="evV")
                banks.release(bnk, "evV", vv)
            kb.wait("scalar", "mm", m)
            sl = slice(j * TL + n * NF, j * TL + (n + 1) * NF)
            sv = kb.inc("evS")
            kb.op("scalar", lambda e, bnk=bnk, sl=sl: e.activation(
                out=ybuf[:, sl], in_=pb[bnk][:, :], func=AF.Copy).then_inc(kb.h["evS"], 1))
            banks.release(bnk, "evS", sv)

        phase_matmuls(wv_t, v_consume, pre=kcol_step)
        m_v_done = kb.cur("mm")
        e_v_sig = kb.cur("evS")
        if attnsub < 4:
            return kb.cur("mm")

        kb.wait("vector", "mm", m_v_done)
        kb.wait("vector", "evS", e_v_sig)
        emit_mix("tmr", cr_t)

        def r_consume(j, n, bnk, m):
            scrs = scr2[n]
            kb.wait("scalar", "mm", m)
            kb.wait("scalar", "evV", scr2_cred[n])
            sv = kb.inc("evS")
            kb.op("scalar", lambda e, bnk=bnk, scrs=scrs: e.activation(
                out=scrs[:, :], in_=pb[bnk][:, :], func=AF.Sigmoid).then_inc(kb.h["evS"], 1))
            banks.release(bnk, "evS", sv)
            kb.wait("vector", "evS", sv)
            if n == 0:
                kb.op("vector", lambda e, scrs=scrs, j=j: e.tensor_copy(
                    out=rcol[:, j:j + 1], in_=scrs[:, 0:1]))
            sl = slice(j * TL + n * NF, j * TL + (n + 1) * NF)
            vv = v_op(lambda e, scrs=scrs, sl=sl: e.tensor_tensor(
                out=ybuf[:, sl], in0=ybuf[:, sl], in1=scrs[:, :], op=ALU.mult), inc="evV")
            scr2_cred[n] = vv

        phase_matmuls(wr_t, r_consume)
        if attnsub < 5:
            return kb.cur("mm")

        aat = vperm[:, VOFF["aa"]:VOFF["aa"] + CK]
        bbt = vperm[:, VOFF["bb"]:VOFF["bb"] + CK]
        ppt = vperm[:, VOFF["pp"]:VOFF["pp"] + CK]
        tft = vperm[:, VOFF["tf"]:VOFF["tf"] + CK]
        v_ser(lambda e: e.tensor_tensor(out=cA[:, :], in0=kcol[:, :], in1=tft, op=ALU.add))
        v_ser(lambda e: e.tensor_tensor(out=cB[:, :], in0=cA[:, :], in1=ppt, op=ALU.max))
        v_ser(lambda e: e.tensor_tensor(out=cC[:, :], in0=ppt, in1=cB[:, :], op=ALU.subtract))
        v_ser(lambda e: e.tensor_tensor(out=cA[:, :], in0=cA[:, :], in1=cB[:, :], op=ALU.subtract))
        v_ser(lambda e: e.tensor_scalar(
            out=cC[:, :], in0=cC[:, :], scalar1=-100.0, scalar2=None, op0=ALU.max))
        v_c1 = v_ser(lambda e: e.tensor_scalar(
            out=cA[:, :], in0=cA[:, :], scalar1=-100.0, scalar2=None, op0=ALU.max), inc="rhs")
        kb.wait("scalar", "rhs", v_c1)
        s_exp = kb.inc("scl")
        kb.op("scalar", lambda e: e.activation(out=cC[:, :], in_=cC[:, :], func=AF.Exp))
        kb.op("scalar", lambda e: e.activation(out=cA[:, :], in_=cA[:, :], func=AF.Exp).then_inc(kb.h["scl"], 1))
        kb.wait("vector", "scl", s_exp)
        v_ser(lambda e: e.tensor_tensor(out=cB[:, :], in0=cC[:, :], in1=aat, op=ALU.mult))
        v_ser(lambda e: e.tensor_tensor(out=cD[:, :], in0=cA[:, :], in1=vcol[:, :], op=ALU.mult))
        v_ser(lambda e: e.tensor_tensor(out=cB[:, :], in0=cB[:, :], in1=cD[:, :], op=ALU.add))
        v_ser(lambda e: e.tensor_tensor(out=cC[:, :], in0=cC[:, :], in1=bbt, op=ALU.mult))
        v_ser(lambda e: e.tensor_tensor(out=cC[:, :], in0=cC[:, :], in1=cA[:, :], op=ALU.add))
        v_ser(lambda e: e.reciprocal(out=cC[:, :], in_=cC[:, :]))
        v_ser(lambda e: e.tensor_tensor(out=cB[:, :], in0=cB[:, :], in1=cC[:, :], op=ALU.mult))
        v_ser(lambda e: e.tensor_tensor(out=cB[:, :], in0=cB[:, :], in1=rcol[:, :], op=ALU.mult))
        for j in range(CK):
            v_ser(lambda e, j=j: e.tensor_copy(out=ybuf[:, j * TL:j * TL + 1], in_=cB[:, j:j + 1]),
                  inc="rhs" if j == CK - 1 else None)
        if attnsub < 6:
            return kb.cur("mm")

        def o_consume(j, n, bnk, m):
            kb.wait("vector", "mm", m)
            sl = slice(j * TL + n * NF, j * TL + (n + 1) * NF)
            vv = v_op(lambda e, bnk=bnk, sl=sl: e.tensor_tensor(
                out=xT[:, sl], in0=xT[:, sl], in1=pb[bnk][:, :], op=ALU.add), inc="evV")
            banks.release(bnk, "evV", vv)

        def out_matmuls():
            for j in range(CK):
                slot, vw = ringA_load(wo_t[j])
                kb.wait("tensor", vw[0], vw[1])
                if j == 0:
                    kb.wait("tensor", "rhs", kb.cur("rhs"))
                bA = (2 * j) % 4
                bB = (2 * j + 1) % 4
                banks.acquire(bA)
                banks.acquire(bB)
                m = None
                for c in range(CK):
                    t_mm(pb[bA][:, :], slot[:, c * 128:(c + 1) * 128],
                         ybuf[:, c * TL:c * TL + NF], c == 0, c == CK - 1)
                    m = t_mm(pb[bB][:, :], slot[:, c * 128:(c + 1) * 128],
                             ybuf[:, c * TL + NF:c * TL + 2 * NF],
                             c == 0, c == CK - 1, inc="mm" if c == CK - 1 else None) or m
                o_consume(j, 0, bA, m)
                o_consume(j, 1, bB, m)
                ringA_consumed()

        if dbg and stages == 2:
            kb.sem("std")
            kb.wait("gpsimd", "rhs", kb.cur("rhs"))
            kb.wait("gpsimd", "evS", kb.cur("evS"))
            dma("gpsimd", o_dbg[2][:, :], ybuf[:, :], "std")
            dma("gpsimd", o_dbg[0][:, 0:CK], kcol[:, :], "std")
            dma("gpsimd", o_dbg[0][:, CK:2 * CK], vcol[:, :], "std")
            dma("gpsimd", o_dbg[0][:, 2 * CK:3 * CK], rcol[:, :], "std")
            dma("gpsimd", o_dbg[0][:, 3 * CK:4 * CK], mixcol_b[:, :], "std")
            dma("gpsimd", o_dbg[0][:, 4 * CK:5 * CK], cB[:, :], "std")
            dma("gpsimd", o_dbg[1][:, :], xn[:, :], "std")
            dma("gpsimd", o_dbg[0][:, 5 * CK:6 * CK], ck_t[:, :], "std")
            dma("gpsimd", o_dbg[0][:, 6 * CK:7 * CK], cv_t[:, :], "std")
            dma("gpsimd", o_dbg[0][:, 7 * CK:8 * CK], cr_t[:, :], "std")
            dma("gpsimd", o_dbg[0][:, 128:256], vperm[:, 0:128], "std")
            kb.wait("gpsimd", "std", kb.cur("std"))
        out_matmuls()
        return kb.cur("mm")

    def emit_ffn(layer, v_xn, m_ws_free, stats_hook=None):
        sgs_o = VOFF[f"sgs{layer}"]
        sus_o = VOFF[f"sus{layer}"]
        sds_o = VOFF[f"sds{layer}"]
        wgL, wuL, wdL = wg[layer], wu[layer], wd[layer]

        kb.wait("tensor", "rhs", v_xn)
        banks.acquire(6)
        banks.acquire(7)
        m_cs = None
        for c in range(CK):
            t_mm(pb[6][0:1, :], ones_col_b[:, :], xn[:, c * TL:c * TL + NF],
                 c == 0, c == CK - 1)
            m_cs = t_mm(pb[7][0:1, :], ones_col_b[:, :], xn[:, c * TL + NF:c * TL + 2 * NF],
                        c == 0, c == CK - 1, inc="mm" if c == CK - 1 else None) or m_cs
        kb.wait("scalar", "mm", m_cs)
        s_cs = None
        for n in range(NT):
            if n == NT - 1:
                s_cs = kb.inc("scl")
                kb.op("scalar", lambda e, n=n: e.activation(
                    out=csrow[0:1, n * NF:(n + 1) * NF], in_=pb[6 + n][0:1, :],
                    func=AF.Copy, scale=-128.0).then_inc(kb.h["scl"], 1))
            else:
                kb.op("scalar", lambda e, n=n: e.activation(
                    out=csrow[0:1, n * NF:(n + 1) * NF], in_=pb[6 + n][0:1, :],
                    func=AF.Copy, scale=-128.0))
        banks.release(6, "scl", s_cs)
        banks.release(7, "scl", s_cs)
        kb.wait("tensor", "scl", s_cs)
        banks.acquire(4)
        banks.acquire(5)
        t_mm(pb[4][:, :], ones_row_b[:, :], csrow[0:1, 0:NF], True, True)
        m_csb = t_mm(pb[5][:, :], ones_row_b[:, :], csrow[0:1, NF:2 * NF], True, True, inc="mm")
        kb.wait("vector", "mm", m_csb)
        kb.op("vector", lambda e: e.tensor_copy(out=csb[:, 0:NF], in_=pb[4][:, :]))
        v_csb = v_op(lambda e: e.tensor_copy(out=csb[:, NF:2 * NF], in_=pb[5][:, :]), inc="evV")
        banks.release(4, "evV", v_csb)
        banks.release(5, "evV", v_csb)

        for g in range(NG):
            for ic in range(GI):
                j = g * GI + ic
                slot_g, vwg = ringA_load(wgL[j])
                slot_u, vwu = ringA_load(wuL[j])
                if ic == 0 and g == 0:
                    kb.wait("vector", "mm", m_ws_free)
                kb.wait("tensor", vwg[0], vwg[1])
                gbanks = [4 * (ic % 2), 4 * (ic % 2) + 1]
                banks.acquire(gbanks[0])
                banks.acquire(gbanks[1])
                m_g = None
                for c in range(CK):
                    t_mm(pb[gbanks[0]][:, :], slot_g[:, c * 128:(c + 1) * 128],
                         xn[:, c * TL:c * TL + NF], c == 0, c == CK - 1)
                    m_g = t_mm(pb[gbanks[1]][:, :], slot_g[:, c * 128:(c + 1) * 128],
                               xn[:, c * TL + NF:c * TL + 2 * NF], c == 0, c == CK - 1,
                               inc="mm" if c == CK - 1 else None) or m_g
                ringA_consumed()
                kb.wait("tensor", vwu[0], vwu[1])
                ubanks = [4 * (ic % 2) + 2, 4 * (ic % 2) + 3]
                banks.acquire(ubanks[0])
                banks.acquire(ubanks[1])
                m_u = None
                for c in range(CK):
                    t_mm(pb[ubanks[0]][:, :], slot_u[:, c * 128:(c + 1) * 128],
                         xn[:, c * TL:c * TL + NF], c == 0, c == CK - 1)
                    m_u = t_mm(pb[ubanks[1]][:, :], slot_u[:, c * 128:(c + 1) * 128],
                               xn[:, c * TL + NF:c * TL + 2 * NF], c == 0, c == CK - 1,
                               inc="mm" if c == CK - 1 else None) or m_u
                ringA_consumed()
                for n in range(NT):
                    gbn, ubn = gbanks[n], ubanks[n]
                    scrs = scr2[n]
                    nsl = slice(n * NF, (n + 1) * NF)
                    kb.wait("vector", "mm", m_g)
                    vs_g = v_op(lambda e, gbn=gbn, nsl=nsl: e.tensor_tensor(
                        out=pb[gbn][:, :], in0=pb[gbn][:, :], in1=csb[:, nsl], op=ALU.add),
                        inc="evV")
                    kb.wait("scalar", "mm", m_g)
                    kb.wait("scalar", "evV", max(vs_g, scr2_cred[n]))
                    sv = kb.inc("evS")
                    kb.op("scalar", lambda e, gbn=gbn, scrs=scrs, j=j: e.activation(
                        out=scrs[:, :], in_=pb[gbn][:, :], func=AF.Silu,
                        scale=vperm[:, sgs_o + j:sgs_o + j + 1]).then_inc(kb.h["evS"], 1))
                    banks.release(gbn, "evS", sv)
                    kb.wait("vector", "mm", m_u)
                    vs_u = v_op(lambda e, ubn=ubn, nsl=nsl: e.tensor_tensor(
                        out=pb[ubn][:, :], in0=pb[ubn][:, :], in1=csb[:, nsl], op=ALU.add),
                        inc="evV")
                    kb.wait("vector", "evS", sv)
                    kb.wait("vector", "evV", vs_u)
                    sl = slice(ic * TL + n * NF, ic * TL + (n + 1) * NF)
                    vv = v_op(lambda e, ubn=ubn, scrs=scrs, sl=sl, j=j: e.scalar_tensor_tensor(
                        out=hbuf[:, sl], in0=pb[ubn][:, :], scalar=vperm[:, sus_o + j:sus_o + j + 1],
                        in1=scrs[:, :], op0=ALU.mult, op1=ALU.mult), inc="evV")
                    scr2_cred[n] = vv
                    banks.release(ubn, "evV", vv)
            v_h = kb.cur("evV")

            banks.acquire(6)
            banks.acquire(7)
            kb.wait("tensor", "evV", v_h)
            m_hs = None
            for ic in range(GI):
                t_mm(pb[6][0:1, :], ones_col_b[:, :], hbuf[:, ic * TL:ic * TL + NF],
                     ic == 0, ic == GI - 1)
                m_hs = t_mm(pb[7][0:1, :], ones_col_b[:, :], hbuf[:, ic * TL + NF:ic * TL + 2 * NF],
                            ic == 0, ic == GI - 1, inc="mm" if ic == GI - 1 else None) or m_hs
            kb.wait("vector", "mm", m_hs)
            v_hsr = None
            for n in range(NT):
                v_ser(lambda e, n=n: e.tensor_scalar(
                    out=hs_hi[0:1, n * NF:(n + 1) * NF], in0=pb[6 + n][0:1, :],
                    scalar1=-128.0, scalar2=None, op0=ALU.mult))
                v_hsr = v_ser(lambda e, n=n: e.scalar_tensor_tensor(
                    out=hs_lo[0:1, n * NF:(n + 1) * NF], in0=pb[6 + n][0:1, :], scalar=-128.0,
                    in1=hs_hi[0:1, n * NF:(n + 1) * NF], op0=ALU.mult, op1=ALU.subtract),
                    inc="evV" if n == NT - 1 else None) or v_hsr
            banks.release(6, "evV", v_hsr)
            banks.release(7, "evV", v_hsr)
            kb.wait("tensor", "evV", v_hsr)

            for o in range(16):
                slot, vw = ringD_load(wdL[g, o])
                kb.wait("tensor", vw[0], vw[1])
                bA = (2 * o) % 4
                bB = (2 * o + 1) % 4
                banks.acquire(bA)
                banks.acquire(bB)
                for ic in range(GI):
                    t_mm(pb[bA][:, :], slot[:, ic * 128:(ic + 1) * 128],
                         hbuf[:, ic * TL:ic * TL + NF], ic == 0, False)
                    t_mm(pb[bB][:, :], slot[:, ic * 128:(ic + 1) * 128],
                         hbuf[:, ic * TL + NF:ic * TL + 2 * NF], ic == 0, False)
                t_mm(pb[bA][:, :], ones_row_b[:, :], hs_hi[0:1, 0:NF], False, False)
                t_mm(pb[bB][:, :], ones_row_b[:, :], hs_hi[0:1, NF:2 * NF], False, False)
                t_mm(pb[bA][:, :], ones_row_b[:, :], hs_lo[0:1, 0:NF], False, True)
                m = t_mm(pb[bB][:, :], ones_row_b[:, :], hs_lo[0:1, NF:2 * NF],
                         False, True, inc="mm")
                kb.wait("vector", "mm", m)
                for n, bnk in ((0, bA), (1, bB)):
                    sl = slice(o * TL + n * NF, o * TL + (n + 1) * NF)
                    vv = v_op(lambda e, bnk=bnk, sl=sl, o=o: e.scalar_tensor_tensor(
                        out=xT[:, sl], in0=pb[bnk][:, :], scalar=vperm[:, sds_o + o:sds_o + o + 1],
                        in1=xT[:, sl], op0=ALU.mult, op1=ALU.add), inc="evV")
                    banks.release(bnk, "evV", vv)
                if stats_hook is not None and g == NG - 1:
                    stats_hook(o)
                ringD_consumed()
        return kb.cur("mm")

    def emit_l1_attn(m_ws_free):
        ise = vcolv("iseven")
        usc = vcolv("usec")
        m_q = {}
        m_g = {}
        qb = {}
        gb = {}
        sig_done = {}
        slot_free = {}

        def emit_chunk_matmuls(i):
            slot, vw = ringA_load(w1_t[i])
            kb.wait("tensor", vw[0], vw[1])
            qbanks = [4 * (i % 2), 4 * (i % 2) + 1]
            banks.acquire(qbanks[0])
            banks.acquire(qbanks[1])
            for c in range(CK):
                t_mm(pb[qbanks[0]][:, :], slot[:, c * 128:(c + 1) * 128],
                     xn[:, c * TL:c * TL + NF], c == 0, c == CK - 1)
                m_q[i] = t_mm(pb[qbanks[1]][:, :], slot[:, c * 128:(c + 1) * 128],
                              xn[:, c * TL + NF:c * TL + 2 * NF], c == 0, c == CK - 1,
                              inc="mm" if c == CK - 1 else None) or m_q.get(i)
            qb[i] = qbanks
            ringA_consumed()
            slot, vw = ringA_load(w1_t[16 + i])
            kb.wait("tensor", vw[0], vw[1])
            gbanks = [4 * (i % 2) + 2, 4 * (i % 2) + 3]
            banks.acquire(gbanks[0])
            banks.acquire(gbanks[1])
            for c in range(CK):
                t_mm(pb[gbanks[0]][:, :], slot[:, c * 128:(c + 1) * 128],
                     xn[:, c * TL:c * TL + NF], c == 0, c == CK - 1)
                m_g[i] = t_mm(pb[gbanks[1]][:, :], slot[:, c * 128:(c + 1) * 128],
                              xn[:, c * TL + NF:c * TL + 2 * NF], c == 0, c == CK - 1,
                              inc="mm" if c == CK - 1 else None) or m_g.get(i)
            gb[i] = gbanks
            ringA_consumed()

        def emit_chunk_sig(i):
            s = i % 8
            gbanks = gb[i]
            kb.wait("scalar", "mm", m_g[i])
            if i >= 8:
                kb.wait("scalar", "evV", slot_free[s])
            sg = None
            for n in range(NT):
                if n == NT - 1:
                    sg = kb.inc("scl")
                    kb.op("scalar", lambda e, bnk=gbanks[n], s=s, n=n: e.activation(
                        out=slotg[:, s * TL + n * NF:s * TL + (n + 1) * NF],
                        in_=pb[bnk][:, :], func=AF.Sigmoid).then_inc(kb.h["scl"], 1))
                else:
                    kb.op("scalar", lambda e, bnk=gbanks[n], s=s, n=n: e.activation(
                        out=slotg[:, s * TL + n * NF:s * TL + (n + 1) * NF],
                        in_=pb[bnk][:, :], func=AF.Sigmoid))
            sig_done[i] = sg
            for n in range(NT):
                banks.release(gbanks[n], "scl", sg)

        def emit_chunk_evac(i):
            s = i % 8
            kb.wait("vector", "mm", m_q[i])
            vq = None
            for n in range(NT):
                vq = v_op(lambda e, bnk=qb[i][n], s=s, n=n: e.tensor_scalar(
                    out=slotq[:, s * TL + n * NF:s * TL + (n + 1) * NF], in0=pb[bnk][:, :],
                    scalar1=1e-20, scalar2=None, op0=ALU.max),
                    inc="evV" if n == NT - 1 else None) or vq
            for n in range(NT):
                banks.release(qb[i][n], "evV", vq)
            kb.wait("vector", "scl", sig_done[i])
            kb.op("vector", lambda e, s=s: e.tensor_scalar(
                out=slotg[:, s * TL:(s + 1) * TL], in0=slotg[:, s * TL:(s + 1) * TL],
                scalar1=1e-20, scalar2=None, op0=ALU.max))
            v_ser(lambda e, s=s: e.tensor_tensor_scan(
                out=scanscr[:, :], data0=slotg[:, s * TL:(s + 1) * TL],
                data1=slotq[:, s * TL:(s + 1) * TL], initial=0.0,
                op0=ALU.mult, op1=ALU.add))
            v_ser(lambda e, i=i: e.tensor_copy(
                out=lastcols[:, i:i + 1], in_=scanscr[:, TL - 1:TL]))

        def emit_scan2(i, cbf):
            s = i % 8
            i8 = i % 8
            v_ser(lambda e, s=s, cbf=cbf, i8=i8: e.tensor_tensor_scan(
                out=scanscr2[:, :], data0=slotg[:, s * TL:(s + 1) * TL],
                data1=slotq[:, s * TL:(s + 1) * TL], initial=cbf[:, i8:i8 + 1],
                op0=ALU.mult, op1=ALU.add))
            v_fin = v_op(lambda e, i=i: e.tensor_tensor(
                out=xT[:, i * TL:(i + 1) * TL], in0=xT[:, i * TL:(i + 1) * TL],
                in1=scanscr2[:, :], op=ALU.add), inc="evV")
            slot_free[s] = v_fin
            kb.wait("sync", "evV", v_fin)
            dma("sync", o_x[:, i * TL:(i + 1) * TL], xT[:, i * TL:(i + 1) * TL], "st")

        kb.wait("vector", "mm", m_ws_free)
        for i in range(8):
            emit_chunk_matmuls(i)
            emit_chunk_sig(i)
            emit_chunk_evac(i)
        v_cc1 = v_ser(lambda e: e.tensor_scalar(
            out=ccsb1[:, :], in0=lastcols[:, 0:8], scalar1=ise, scalar2=None, op0=ALU.mult),
            inc="vcc")
        emit_chunk_matmuls(8)
        emit_chunk_matmuls(9)
        kb.wait("gpsimd", "vcc", v_cc1)
        vd = dma("gpsimd", cc_in1[:, :], ccsb1[:, :], "wq")
        kb.wait("gpsimd", "wq", vd)
        c1 = kb.inc("ccs")
        kb.op("gpsimd", lambda e: e.collective_compute(
            "AllReduce", ALU.add, replica_groups=[[0, 1], [2, 3], [4, 5], [6, 7]],
            ins=[cc_in1.ap()], outs=[cc_out1.ap()]).then_inc(kb.h["ccs"], 1))
        kb.wait("gpsimd", "ccs", c1)
        v_r1 = dma("gpsimd", ccres1[:, :], cc_out1[:, :], "wq")
        kb.wait("vector", "wq", v_r1)
        v_ser(lambda e: e.tensor_scalar(
            out=cbuf1[:, :], in0=ccres1[:, :], scalar1=usc, scalar2=None, op0=ALU.mult))
        for k in range(8):
            emit_scan2(k, cbuf1)
            emit_chunk_sig(8 + k)
            emit_chunk_evac(8 + k)
            if 10 + k <= 15:
                emit_chunk_matmuls(10 + k)
        v_cc2 = v_ser(lambda e: e.tensor_scalar(
            out=ccsb2[:, :], in0=lastcols[:, 8:16], scalar1=ise, scalar2=None, op0=ALU.mult),
            inc="vcc")
        kb.wait("gpsimd", "vcc", v_cc2)
        vd = dma("gpsimd", cc_in2[:, :], ccsb2[:, :], "wq")
        kb.wait("gpsimd", "wq", vd)
        c2 = kb.inc("ccs")
        kb.op("gpsimd", lambda e: e.collective_compute(
            "AllReduce", ALU.add, replica_groups=[[0, 1], [2, 3], [4, 5], [6, 7]],
            ins=[cc_in2.ap()], outs=[cc_out2.ap()]).then_inc(kb.h["ccs"], 1))
        kb.wait("gpsimd", "ccs", c2)
        v_r2 = dma("gpsimd", ccres2[:, :], cc_out2[:, :], "wq")
        kb.wait("vector", "wq", v_r2)
        v_ser(lambda e: e.tensor_scalar(
            out=cbuf2[:, :], in0=ccres2[:, :], scalar1=usc, scalar2=None, op0=ALU.mult))
        for k in range(8):
            emit_scan2(8 + k, cbuf2)

    def emit_dbg_store(idx):
        kb.wait("sync", "evV", kb.cur("evV"))
        kb.wait("sync", "rhs", kb.cur("rhs"))
        for q in range(4):
            dma("sync", o_dbg[idx][:, q * 4096:(q + 1) * 4096], xT[:, q * 4096:(q + 1) * 4096], "st")
        kb.wait("vector", "st", kb.cur("st"))

    if stages >= 1:
        v_xn0 = emit_layernorm(0)
    if stages >= 2:
        m_free0 = emit_l0_attn()
        if dbg and stages != 2:
            emit_dbg_store(0)
    if stages >= 3:
        m_ffn0 = emit_ffn(0, v_xn0, m_free0,
                          stats_hook=ln_stats_hook if stages >= 4 else None)
        if dbg:
            emit_dbg_store(1)
    if stages >= 4:
        kb.wait("vector", "mm", m_ffn0)
        v_xn1 = emit_layernorm(1, fused=True)
    if stages >= 5:
        m_ffn1 = emit_ffn(1, v_xn1, kb.cur("mm"))
        if dbg:
            emit_dbg_store(2)
    if stages >= 6:
        emit_l1_attn(m_ffn1)
    else:
        kb.wait("sync", "evV", kb.cur("evV"))
        kb.wait("sync", "rhs", kb.cur("rhs"))
        kb.wait("sync", "evS", kb.cur("evS"))
        kb.wait("sync", "mm", kb.cur("mm"))
        for q in range(4):
            dma("sync", o_x[:, q * 4096:(q + 1) * 4096], xT[:, q * 4096:(q + 1) * 4096], "st")

    kb.wait("sync", "st", kb.cur("st"))
    kb.wait("gpsimd", "st", kb.cur("st"))

    with contextlib.ExitStack() as es:
        for s in kb.cnt:
            kb.h[s] = es.enter_context(nc.semaphore(s))
        block = es.enter_context(nc.Block())
        for eng in ENGINES:
            ops = kb.ops[eng]

            def runner(e, ops=ops):
                for f in ops:
                    f(e)
            getattr(block, eng)(runner)

    return nc


def _host_layout(inputs):
    x = np.asarray(inputs["L0_attn_x"])

    def tile_w(w, oc):
        return np.ascontiguousarray(
            np.asarray(w).reshape(oc, 128, 16, 128).transpose(0, 3, 2, 1)).reshape(oc, 128, 2048)

    wk = tile_w(inputs["L0_attn_w_key"], 16)
    wv = tile_w(inputs["L0_attn_w_value"], 16)
    wr = tile_w(inputs["L0_attn_w_rec"], 16)
    wo = tile_w(inputs["L0_attn_w_out"], 16)
    w1 = tile_w(inputs["L1_attn_W"], 32)

    def tile_d(w):
        return np.ascontiguousarray(
            np.asarray(w).reshape(16, 128, NG, GI, 128).transpose(2, 0, 4, 3, 1)).reshape(NG, 16, 128, GI * 128)

    wgs = [tile_w(inputs["L0_ffn_gate_w"], 64), tile_w(inputs["L1_ffn_gate_w"], 64)]
    wus = [tile_w(inputs["L0_ffn_up_w"], 64), tile_w(inputs["L1_ffn_up_w"], 64)]
    wds = [tile_d(inputs["L0_ffn_down_w"]), tile_d(inputs["L1_ffn_down_w"])]

    def v16(a):
        return np.asarray(a).reshape(16, 128).T

    base = np.zeros((128, VW), dtype=np.float32)
    base[:, VOFF["tf"]:VOFF["tf"] + 16] = v16(inputs["L0_attn_time_first"])
    base[:, VOFF["tmk"]:VOFF["tmk"] + 16] = v16(inputs["L0_attn_time_mix_k"])
    base[:, VOFF["tmv"]:VOFF["tmv"] + 16] = v16(inputs["L0_attn_time_mix_v"])
    base[:, VOFF["tmr"]:VOFF["tmr"] + 16] = v16(inputs["L0_attn_time_mix_r"])
    base[:, VOFF["xx"]:VOFF["xx"] + 16] = v16(inputs["L0_attn_xx"])
    for l in range(2):
        base[:, VOFF[f"sgs{l}"]:VOFF[f"sgs{l}"] + 64] = np.asarray(inputs[f"L{l}_ffn_gate_scale"]).reshape(64, 128).T
        base[:, VOFF[f"sus{l}"]:VOFF[f"sus{l}"] + 64] = np.asarray(inputs[f"L{l}_ffn_up_scale"]).reshape(64, 128).T
        base[:, VOFF[f"sds{l}"]:VOFF[f"sds{l}"] + 16] = v16(inputs[f"L{l}_ffn_down_scale"])

    in_maps = []
    for c in range(NCORES):
        b, h = c // 2, c % 2
        xs = np.ascontiguousarray(
            np.asarray(x[b, h * TL:(h + 1) * TL, :]).reshape(TL, 16, 128).transpose(2, 1, 0)).reshape(128, 16 * TL)
        v = base.copy()
        if h == 0:
            v[:, VOFF["aa"]:VOFF["aa"] + 16] = v16(inputs["L0_attn_aa"])
            v[:, VOFF["bb"]:VOFF["bb"] + 16] = v16(inputs["L0_attn_bb"])
            v[:, VOFF["pp"]:VOFF["pp"] + 16] = v16(inputs["L0_attn_pp"])
            v[:, VOFF["iseven"]] = 1.0
            v[:, VOFF["usec"]] = 0.0
        else:
            v[:, VOFF["iseven"]] = 0.0
            v[:, VOFF["usec"]] = 1.0
        in_maps.append(dict(
            x_in=xs, vecs=v, wk_t=wk, wv_t=wv, wr_t=wr, wo_t=wo, w1_t=w1,
            wg0=wgs[0], wu0=wus[0], wd0=wds[0],
            wg1=wgs[1], wu1=wus[1], wd1=wds[1]))
    return in_maps


def kernel(**inputs):
    in_maps = _host_layout(inputs)
    nc = build_program(dbg=False)
    res = run_bass_kernel_spmd(nc, in_maps, core_ids=list(range(NCORES)))
    out = np.empty((B, T, C), dtype=np.float32)
    for c in range(NCORES):
        b, h = c // 2, c % 2
        arr = np.asarray(res.results[c]["o_x"]).reshape(128, 16, TL).transpose(2, 1, 0).reshape(TL, C)
        out[b, h * TL:(h + 1) * TL, :] = arr
    return out
